# version 07
# baseline (speedup 1.0000x reference)
import math

import ml_dtypes
import numpy as np

import concourse.bass as bass
import concourse.mybir as mybir
from concourse.bass_utils import run_bass_kernel_spmd

F32 = mybir.dt.float32
BF16 = mybir.dt.bfloat16
AF = mybir.ActivationFunctionType
ALU = mybir.AluOpType
AX = mybir.AxisListType

NCORES = 8
BLOC = 2
S = 2048
D = 1024
NDC = D // 128
NH = 16
HD = 64
NPAIR = NH // 2
E = 8
H = 2048
NHC = H // 128
NB = 32
MAXD = 128
ISQ8 = float(1.0 / math.sqrt(8.0))
EPS_LN = 1e-5
BW = 90
BANDW = 2 * BW + 128
GLEN = 2 * S - 1


def _bucket_of_rel():
    rel = np.arange(-(S - 1), S, dtype=np.int64)
    nb = NB // 2
    b = np.where(rel > 0, nb, 0)
    ar = np.abs(rel)
    me = nb // 2
    small = ar < me
    tmp = np.log(np.maximum(ar, 1).astype(np.float32) / np.float32(me))
    large = (tmp / np.float32(math.log(MAXD / me)) * np.float32(nb - me)).astype(np.int32)
    large = me + large
    large = np.minimum(large, nb - 1)
    return (b + np.where(small, ar, large)).astype(np.int32)


class Sem:
    __slots__ = ("h", "n")

    def __init__(self, h):
        self.h = h
        self.n = 0


class Plan:
    ENGS = ("tensor", "vector", "scalar", "sync", "gpsimd")

    def __init__(self):
        self.ops = {e: [] for e in self.ENGS}

    def op(self, eng, fn, wait=(), inc=None, inc_n=1):
        for s, v in wait:
            assert v <= s.n, f"deadlock: wait {v} > planned {s.n} on {eng}"
        self.ops[eng].append((tuple((s.h, v) for s, v in wait if v > 0), fn,
                              inc.h if inc is not None else None, inc_n))
        if inc is not None:
            inc.n += inc_n
        return inc.n if inc is not None else None

    def emitter(self, eng_name):
        ops = self.ops[eng_name]

        def emit(eng):
            waited = {}
            for waits, fn, inch, incn in ops:
                for h, v in waits:
                    if waited.get(id(h), -1) < v:
                        eng.wait_ge(h, v)
                        waited[id(h)] = v
                inst = fn(eng)
                if inch is not None:
                    inst.then_inc(inch, incn)
        return emit


def build_nc(stage="full"):
    nc = bass.Bass("TRN2", target_bir_lowering=False,
                   detect_race_conditions=False)

    x_d = nc.dram_tensor("x", [BLOC, S, D], F32, kind="ExternalInput").ap()
    wqkv_d = nc.dram_tensor("wqkv", [D, 3 * D], F32, kind="ExternalInput").ap()
    bqkv_d = nc.dram_tensor("bqkv", [3 * D], F32, kind="ExternalInput").ap()
    wd_d = nc.dram_tensor("wdense", [D, D], F32, kind="ExternalInput").ap()
    bd_d = nc.dram_tensor("bdense", [D], F32, kind="ExternalInput").ap()
    tbl_d = nc.dram_tensor("table", [NH, NB], F32, kind="ExternalInput").ap()
    wg_d = nc.dram_tensor("wg", [D, E], F32, kind="ExternalInput").ap()
    w1_d = nc.dram_tensor("w1", [E, D, H], F32, kind="ExternalInput").ap()
    w2_d = nc.dram_tensor("w2", [E, H, D], F32, kind="ExternalInput").ap()
    oh_d = nc.dram_tensor("oh", [NB, GLEN], F32, kind="ExternalInput").ap()
    ones_d = nc.dram_tensor("cones", [128, 128], BF16, kind="ExternalInput").ap()
    idb_d = nc.dram_tensor("cidb", [128, 128], BF16, kind="ExternalInput").ap()
    idf_d = nc.dram_tensor("cidf", [128, 128], F32, kind="ExternalInput").ap()
    onesf_d = nc.dram_tensor("conesf", [128, 128], F32, kind="ExternalInput").ap()
    sel8_d = nc.dram_tensor("csel8", [E, E * 128], BF16, kind="ExternalInput").ap()
    y_d = nc.dram_tensor("y", [BLOC, S, D], F32, kind="ExternalOutput").ap()
    gscr = nc.dram_tensor("gscr", [NH, GLEN], F32, kind="Internal").ap()

    def alloc(name, shape, dtype, off):
        assert off % 32 == 0, (name, off)
        return nc.alloc_sbuf_tensor_at(name, list(shape), dtype, offset=off).ap()

    def aup(v):
        return (v + 31) & ~31

    BASE = 17024
    R_X, R_U, R_C, R_D = BASE, BASE + 65536, BASE + 98304, BASE + 131072
    R_E, R_F = BASE + 163840, BASE + 180224
    R_G, R_H = BASE + 192512, BASE + 200704
    x_sb = alloc("x_sb", [128, NDC, S], F32, R_X)
    u_sb = alloc("u_sb", [128, NDC, S], BF16, R_U)
    ctx_sb = alloc("ctx_sb", [128, NPAIR, S], BF16, R_C)
    w1sl = alloc("w1sl", [128, NDC, H], BF16, R_C)
    oh_sb = alloc("oh_sb", [NB, GLEN], F32, R_C)
    g_sbuf = alloc("g_sbuf", [NH, GLEN], F32, R_D)
    w2sl = alloc("w2sl", [128, NHC, D], BF16, R_D)
    wdf_sb = alloc("wdf_sb", [128, NPAIR, D], BF16, R_D)
    dD = R_D
    qp_sb = alloc("qp_sb", [128, S], BF16, dD); dD += 4096
    kpz_sb = []
    for hl in range(2):
        kpz_sb.append(alloc(f"kpz{hl}", [128, S], BF16, dD)); dD += 4096
    vp_sb = alloc("vp_sb", [128, S], BF16, dD); dD += 4096
    pt_sb = [[None, None], [None, None]]
    for hl in range(2):
        for par in range(2):
            pt_sb[hl][par] = alloc(f"pt{hl}{par}", [128, 1024], BF16, dD); dD += 2048
    vaug_sb = []
    for hl in range(2):
        vaug_sb.append(alloc(f"vaug{hl}", [128, 16 * 65], BF16, dD)); dD += 2176
    band_sb = []
    for hl in range(2):
        band_sb.append(alloc(f"band{hl}", [128, BANDW], F32, dD)); dD += 1280
    ctxu2 = [alloc("ctxu0", [128, 1024], BF16, R_G + 4096),
             alloc("ctxu1", [128, 1024], BF16, R_D + 16384)]
    ctxtm2 = [alloc("ctxtm0", [128, 512], BF16, R_G + 4096 + 2048),
              alloc("ctxtm1", [128, 512], BF16, R_D + 16384 + 2048)]
    assert dD - R_D <= 32768
    dL = R_D
    m_row = alloc("m_row", [1, 1024], F32, dL); dL += 4096
    v_row = alloc("v_row", [1, 1024], F32, dL); dL += 4096
    t_row = alloc("t_row", [1, 1024], F32, dL); dL += 4096
    sq_sb = alloc("sq_sb", [128, S], BF16, dL); dL += 4096
    logits_sb = alloc("logits_sb", [E, S], F32, dL); dL += 8192
    assert dL - R_D <= 32768
    gg_sb = alloc("gg_sb", [128, NHC, 512], BF16, R_E)
    dE = R_E
    lg_tm = alloc("lg_tm", [128, 128], F32, dE); dE += 512
    pexp_sb = alloc("pexp_sb", [128, 128], F32, dE); dE += 512
    m1_sb = alloc("m1_sb", [128, 16], F32, dE); dE += 64
    m2_sb = alloc("m2_sb", [128, 16], F32, dE); dE += 64
    den_sb = alloc("den_sb", [128, 16], F32, dE); dE += 64
    rden_sb = alloc("rden_sb", [128, 16], F32, dE); dE += 64
    keep_sb = alloc("keep_sb", [128, 128], F32, dE); dE += 512
    q2_sb = alloc("q2_sb", [128, 128], F32, dE); dE += 512
    cval_sb = alloc("cval_sb", [128, 128], F32, dE); dE += 512
    sqctm_sb = alloc("sqctm_sb", [128, 128], BF16, dE); dE += 256
    rz2 = [alloc(f"rz{i}", [128, 8], F32, dE + 32 * i) for i in range(2)]
    dE += 64
    assert dE - R_E <= 16384
    wqkv_sb = [alloc(f"wqkv{i}", [128, NDC, 3, 128], BF16, R_F + i * 6144)
               for i in range(2)]
    sqcb_sb = alloc("sqcb_sb", [128, S], BF16, R_F)
    sqce_sb = alloc("sqce_sb", [E, S], BF16, R_F + 4096)
    trelu_sb2 = [alloc(f"trelu{i}", [128, 512], BF16, R_F + 8192 + i * 1024)
                 for i in range(2)]
    xtm_sb = [alloc(f"xtm{i}", [128, 1024], F32, R_G + i * 4096) for i in range(2)]
    ot_sb = [alloc(f"ot{i}", [128, 1024], F32, R_G + i * 4096) for i in range(2)]
    sq2_sb = alloc("sq2_sb", [128, S], BF16, R_G + 4096)
    ue_sb = alloc("ue_sb", [128, NDC, 512], BF16, R_G)
    dH = R_H
    onesb = alloc("onesb", [128, 128], BF16, dH); dH = (dH + 256 + 31) & ~31
    idb = alloc("idb", [128, 128], BF16, dH); dH = (dH + 256 + 31) & ~31
    idf = alloc("idf", [128, 128], F32, dH); dH = (dH + 512 + 31) & ~31
    onesf = alloc("onesf", [128, 128], F32, dH); dH = (dH + 512 + 31) & ~31
    sel8 = alloc("sel8", [E, E * 128], BF16, dH); dH = (dH + 2048 + 31) & ~31
    bqkv_sb = alloc("bqkv_sb", [128, 24], F32, dH); dH = (dH + 96 + 31) & ~31
    bqs_sb = alloc("bqs_sb", [128, 16], F32, dH); dH = (dH + 64 + 31) & ~31
    bd_sb = alloc("bd_sb", [128, NDC], F32, dH); dH = (dH + 32 + 31) & ~31
    wg_sb = alloc("wg_sb", [128, NDC * E], BF16, dH); dH = (dH + 128 + 31) & ~31
    tbl_sb = alloc("tbl_sb", [NH, NB], F32, dH); dH = (dH + 128 + 31) & ~31
    tblT_sb = alloc("tblT_sb", [NB, NH], F32, dH); dH = (dH + 64 + 31) & ~31
    t15c = alloc("t15c", [128, NH], F32, dH); dH = (dH + 64 + 31) & ~31
    t31c = alloc("t31c", [128, NH], F32, dH); dH = (dH + 64 + 31) & ~31
    trow_sb = alloc("trow_sb", [1, 2 * NH], F32, dH); dH = (dH + 128 + 31) & ~31
    abst_sb = alloc("abst_sb", [NH, NB], F32, dH); dH = (dH + 128 + 31) & ~31
    brow_sb = alloc("brow_sb", [1, NB], F32, dH); dH = (dH + 128 + 31) & ~31
    bmax_sb = alloc("bmax_sb", [1, 1], F32, dH); dH = (dH + 4 + 31) & ~31
    mq2_sb = alloc("mq2_sb", [1, 2], F32, dH); dH = (dH + 8 + 31) & ~31
    mk2_sb = alloc("mk2_sb", [1, 2], F32, dH); dH = (dH + 8 + 31) & ~31
    cc_sb = alloc("cc_sb", [1, 1], F32, dH); dH = (dH + 4 + 31) & ~31
    negc_sb = alloc("negc_sb", [128, 1], F32, dH); dH = (dH + 4 + 31) & ~31
    t31mc = []
    t15mc = []
    for i in range(2):
        t31mc.append(alloc(f"t31mc{i}", [128, 1], F32, dH)); dH = (dH + 4 + 31) & ~31
        t15mc.append(alloc(f"t15mc{i}", [128, 1], F32, dH)); dH = (dH + 4 + 31) & ~31
    assert dH <= 223300, dH

    pq = [nc.alloc_psum_tensor(f"pq{i}", [128, 1024], F32).ap() for i in range(4)]

    fill_sb = alloc("fill_sb", [128, 8], F32, ((dH + 31) & ~31))
    P = Plan()
    sT = Sem(nc.alloc_semaphore("sT"))
    sV = Sem(nc.alloc_semaphore("sV"))
    sS = Sem(nc.alloc_semaphore("sS"))
    sDx = Sem(nc.alloc_semaphore("sDx"))
    sDw = Sem(nc.alloc_semaphore("sDw"))
    dbg = []

    def vgap(n=1):
        for _ in range(n):
            P.op("vector", lambda e: e.memset(fill_sb[:, 0:1], 0.0), inc=sV)

    def dump(name, ap, wait):
        d = nc.dram_tensor("dbg_" + name, list(ap.shape), ap.dtype,
                           kind="ExternalOutput").ap()
        P.op("sync", lambda e, d=d, a=ap: e.dma_start(d, a), wait=wait,
             inc=sDx, inc_n=16)
        dbg.append("dbg_" + name)

    for dst, src in ((onesb, ones_d), (idb, idb_d), (idf, idf_d), (onesf, onesf_d),
                     (sel8, sel8_d), (tbl_sb, tbl_d), (oh_sb, oh_d)):
        P.op("sync", lambda e, d=dst, s=src: e.dma_start(d, s), inc=sDx, inc_n=16)
    P.op("sync", lambda e: e.dma_start(
        bqkv_sb[:], bqkv_d.rearrange("(j p) -> p j", p=128),
        allow_slow_non_contiguous=True), inc=sDx, inc_n=16)
    P.op("sync", lambda e: e.dma_start(
        bd_sb[:], bd_d.rearrange("(j p) -> p j", p=128),
        allow_slow_non_contiguous=True), inc=sDx, inc_n=16)
    P.op("gpsimd", lambda e: e.dma_start(
        wg_sb[:].rearrange("p (j e) -> p j e", e=E),
        wg_d.rearrange("(j p) e -> p j e", p=128)), inc=sDw, inc_n=16)
    pre_ld = sDx.n
    P.op("tensor", lambda e: e.matmul(pq[0][0:NB, 0:NH], tbl_sb[:], idf[0:NH, 0:NH],
                                      start=True, stop=True),
         wait=[(sDx, pre_ld)], inc=sT)
    P.op("vector", lambda e: e.tensor_copy(tblT_sb[:], pq[0][0:NB, 0:NH]),
         wait=[(sT, sT.n)], inc=sV)
    tv0 = sV.n
    for ch in range(8):
        c0, c1 = ch * 512, min(GLEN, ch * 512 + 512)
        P.op("tensor", lambda e, c0=c0, c1=c1, b2=ch % 2: e.matmul(
            pq[2 + b2][0:NH, 0:c1 - c0], tblT_sb[:], oh_sb[:, c0:c1],
            start=True, stop=True), wait=[(sV, tv0 + max(0, ch - 1))], inc=sT)
        P.op("vector", lambda e, c0=c0, c1=c1, b2=ch % 2: e.tensor_copy(
            g_sbuf[:, c0:c1], pq[2 + b2][0:NH, 0:c1 - c0]),
            wait=[(sT, sT.n)], inc=sV)
    P.op("sync", lambda e: e.dma_start(gscr, g_sbuf[:]), wait=[(sV, sV.n)],
         inc=sDx, inc_n=16)
    gscr_done = sDx.n
    P.op("vector", lambda e: e.scalar_tensor_tensor(abst_sb[:], tbl_sb[:], -1.0,
                                                    tbl_sb[:], ALU.mult, ALU.max),
         wait=[(sDx, pre_ld)], inc=sV)
    P.op("tensor", lambda e: e.matmul(pq[0][0:1, 0:NB], onesf[0:NH, 0:1],
                                      abst_sb[:], start=True, stop=True),
         wait=[(sV, sV.n)], inc=sT)
    P.op("vector", lambda e: e.tensor_copy(brow_sb[0:1, 0:NB], pq[0][0:1, 0:NB]),
         wait=[(sT, sT.n)], inc=sV)
    vgap(2)
    P.op("vector", lambda e: e.tensor_reduce(bmax_sb[:], brow_sb[0:1, 0:NB],
                                             AX.X, ALU.add), inc=sV)
    brow_v = sV.n
    P.op("sync", lambda e: e.dma_start(trow_sb[0:1, 0:NH], tblT_sb[15:16, :]),
         wait=[(sV, sV.n)], inc=sDx, inc_n=16)
    P.op("sync", lambda e: e.dma_start(trow_sb[0:1, NH:2 * NH], tblT_sb[31:32, :]),
         inc=sDx, inc_n=16)
    P.op("tensor", lambda e: e.matmul(pq[0][:, 0:NH], onesf[0:1, 0:128],
                                      trow_sb[0:1, 0:NH], start=True, stop=True),
         wait=[(sDx, sDx.n), (sV, brow_v)], inc=sT)
    P.op("tensor", lambda e: e.matmul(pq[1][:, 0:NH], onesf[0:1, 0:128],
                                      trow_sb[0:1, NH:2 * NH], start=True, stop=True),
         inc=sT)
    P.op("vector", lambda e: e.tensor_copy(t15c[:], pq[0][:, 0:NH]),
         wait=[(sT, sT.n)], inc=sV)
    P.op("vector", lambda e: e.tensor_copy(t31c[:], pq[1][:, 0:NH]), inc=sV)
    P.op("vector", lambda e: e.tensor_scalar_mul(bqs_sb[:], bqkv_sb[:, 0:16], ISQ8),
         wait=[(sDx, pre_ld)], inc=sV)
    ot_last_store = [0, 0]

    def emit_ln(extra=()):
        sq_hist = []
        for qh in range(2):
            q0 = qh * 1024
            qh_v = sV.n
            for dc in range(NDC):
                half = dc % 2
                idx = qh * NDC + dc
                w = [(sV, qh_v)] + list(extra)
                if idx >= 2:
                    w.append((sT, sq_hist[idx - 2]))
                P.op("scalar", lambda e, dc=dc, q0=q0, half=half: e.activation(
                    sq_sb[:, half * 1024:(half + 1) * 1024],
                    x_sb[:, dc, q0:q0 + 1024], AF.Square), wait=w, inc=sS)
                sqv = sS.n
                for cc in range(2):
                    P.op("tensor", lambda e, dc=dc, q0=q0, cc=cc: e.matmul(
                        pq[0][0:1, cc * 512:(cc + 1) * 512], onesf[:, 0:1],
                        x_sb[:, dc, q0 + cc * 512:q0 + (cc + 1) * 512],
                        start=(dc == 0), stop=(dc == NDC - 1)),
                        wait=[(sV, qh_v)] if (dc == 0 and cc == 0) else (),
                        inc=None)
                    P.op("tensor", lambda e, half=half, cc=cc, dc=dc: e.matmul(
                        pq[1][0:1, cc * 512:(cc + 1) * 512], onesb[:, 0:1],
                        sq_sb[:, half * 1024 + cc * 512:half * 1024 + (cc + 1) * 512],
                        start=(dc == 0), stop=(dc == NDC - 1)),
                        wait=[(sS, sqv)] if cc == 0 else (),
                        inc=sT if cc == 1 else None)
                sq_hist.append(sT.n)
            tsum = sT.n
            P.op("vector", lambda e: e.tensor_scalar_mul(m_row[:], pq[0][0:1, :],
                                                         1.0 / D),
                 wait=[(sT, tsum)], inc=sV)
            P.op("vector", lambda e: e.tensor_scalar(v_row[:], pq[1][0:1, :],
                                                     1.0 / D, EPS_LN,
                                                     ALU.mult, ALU.add), inc=sV)
            P.op("vector", lambda e: e.tensor_mul(t_row[:], m_row[:], m_row[:]),
                 inc=sV)
            vgap()
            P.op("vector", lambda e: e.tensor_sub(v_row[:], v_row[:], t_row[:]),
                 inc=sV)
            vgap()
            P.op("vector", lambda e: e.reciprocal(t_row[:], v_row[:]), inc=sV)
            vgap(2)
            stat_v = sV.n
            P.op("scalar", lambda e: e.activation(v_row[:], t_row[:], AF.Sqrt),
                 wait=[(sV, stat_v)], inc=sS)
            stat_s = sS.n
            for cc in range(2):
                P.op("tensor", lambda e, cc=cc: e.matmul(
                    pq[0][:, cc * 512:(cc + 1) * 512], onesf[0:1, 0:128],
                    m_row[0:1, cc * 512:(cc + 1) * 512], start=True, stop=True),
                    wait=[(sV, stat_v), (sS, stat_s)] if cc == 0 else (), inc=None)
                P.op("tensor", lambda e, cc=cc: e.matmul(
                    pq[1][:, cc * 512:(cc + 1) * 512], onesf[0:1, 0:128],
                    v_row[0:1, cc * 512:(cc + 1) * 512], start=True, stop=True),
                    inc=sT if cc == 1 else None)
            tb = sT.n
            for dc in range(NDC):
                half = dc % 2
                P.op("vector", lambda e, dc=dc, q0=q0, half=half: e.tensor_sub(
                    sq_sb[:, half * 1024:(half + 1) * 1024],
                    x_sb[:, dc, q0:q0 + 1024], pq[0][:]),
                    wait=[(sT, tb)], inc=sV)
                P.op("vector", lambda e, dc=dc, q0=q0, half=half: e.tensor_mul(
                    u_sb[:, dc, q0:q0 + 1024],
                    sq_sb[:, half * 1024:(half + 1) * 1024], pq[1][:]),
                    inc=sV)

    for b in range(BLOC):
        xtm_free = [0, 0]
        for qch in range(16):
            i = qch % 2
            P.op("sync", lambda e, b=b, qch=qch, i=i: e.dma_start(
                xtm_sb[i][:], x_d[b, qch * 128:(qch + 1) * 128, :]),
                wait=[(sT, xtm_free[i]), (sDx, ot_last_store[i])],
                inc=sDx, inc_n=16)
            ld = sDx.n
            for grp in range(2):
                pb = grp
                for s4 in range(4):
                    dsl = grp * 4 + s4
                    P.op("tensor", lambda e, i=i, dsl=dsl, pb=pb, s4=s4: e.matmul(
                        pq[pb][:, s4 * 128:(s4 + 1) * 128],
                        xtm_sb[i][:, dsl * 128:(dsl + 1) * 128], idf[:],
                        start=True, stop=True),
                        wait=[(sDx, ld), (sV, sV.n)] if s4 == 0 else (),
                        inc=sT if s4 == 3 else None)
                P.op("vector", lambda e, b=b, qch=qch, grp=grp, pb=pb: e.tensor_copy(
                    x_sb[:, grp * 4:(grp + 1) * 4, qch * 128:(qch + 1) * 128],
                    pq[pb][:, 0:512].rearrange("p (c q) -> p c q", c=4)),
                    wait=[(sT, sT.n)], inc=sV)
            xtm_free[i] = sT.n
        if stage == "load":
            dump("x", x_sb[:], [(sV, sV.n)])
            break

        emit_ln(extra=[(sDx, gscr_done)] if b == 0 else ())
        if stage == "ln1":
            dump("u", u_sb[:], [(sV, sV.n)])
            break

        wq_ready = [0, 0]
        wq_user = [0, 0]

        def wqkv_dma(hp, gate):
            i = hp % 2
            for part in range(3):
                P.op("gpsimd", lambda e, hp=hp, i=i, part=part: e.dma_start(
                    wqkv_sb[i][:, :, part, :],
                    bass.AP(wqkv_d.tensor, part * D + hp * 128,
                            [[3 * D, 128], [128 * 3 * D, NDC], [1, 128]])),
                    wait=[(sT, max(wq_user[i], gate))] if part == 0 else (),
                    inc=sDw, inc_n=16)
            wq_ready[i] = sDw.n

        batch_gate = sT.n
        for hl in range(2):
            for kc in range(16):
                P.op("vector", lambda e, hl=hl, kc=kc: e.memset(
                    vaug_sb[hl][:, kc * 65 + 64: kc * 65 + 65], 1.0),
                    wait=[(sT, batch_gate)] if (hl == 0 and kc == 0) else (),
                    inc=sV)
        wqkv_dma(0, batch_gate)
        npair_run = 1 if stage in ("attn1", "attn2", "attn4") else NPAIR
        for hp in range(npair_run):
            i = hp % 2
            if hp + 1 < npair_run:
                wqkv_dma(hp + 1, batch_gate)
            pair_v = sV.n
            P.op("vector", lambda e: e.memset(kpz_sb[0][64:128, :], 0.0),
                 wait=[(sT, sT.n)], inc=sV)
            P.op("vector", lambda e: e.memset(kpz_sb[1][0:64, :], 0.0), inc=sV)
            pair_v = sV.n
            qkv_evac = []
            combo = 0
            for part, dst in ((0, qp_sb), (1, None), (2, vp_sb)):
                for qc in range(4):
                    slot = combo % 4
                    pb, off = slot // 2, (slot % 2) * 512
                    w = [(sDw, wq_ready[i]), (sV, pair_v)]
                    if combo >= 4:
                        w.append((sV, qkv_evac[combo - 4]))
                    for dc in range(NDC):
                        P.op("tensor", lambda e, part=part, qc=qc, dc=dc, pb=pb,
                             off=off, i=i: e.matmul(
                                 pq[pb][:, off:off + 512],
                                 wqkv_sb[i][:, dc, part, :],
                                 u_sb[:, dc, qc * 512:(qc + 1) * 512],
                                 start=(dc == 0), stop=(dc == NDC - 1)),
                             wait=w if dc == 0 else (),
                             inc=sT if dc == NDC - 1 else None)
                    if part == 0:
                        P.op("vector", lambda e, qc=qc, pb=pb, off=off,
                             bcol=bqs_sb[:, hp:hp + 1]: e.tensor_scalar(
                                 qp_sb[:, qc * 512:(qc + 1) * 512],
                                 pq[pb][:, off:off + 512], bcol, ISQ8,
                                 ALU.add, ALU.mult),
                             wait=[(sT, sT.n)], inc=sV)
                    elif part == 1:
                        bcol = bqs_sb[:, 8 + hp:8 + hp + 1]
                        for hl2 in range(2):
                            P.op("vector", lambda e, qc=qc, pb=pb, off=off,
                                 bcol=bcol, hl2=hl2: e.tensor_scalar(
                                     kpz_sb[hl2][hl2 * 64:hl2 * 64 + 64,
                                                 qc * 512:(qc + 1) * 512],
                                     pq[pb][hl2 * 64:hl2 * 64 + 64,
                                            off:off + 512],
                                     bcol[hl2 * 64:hl2 * 64 + 64, :], ISQ8,
                                     ALU.add, ALU.mult),
                                 wait=[(sT, sT.n)] if hl2 == 0 else (), inc=sV)
                    else:
                        P.op("vector", lambda e, qc=qc, pb=pb, off=off,
                             bcol=bqkv_sb[:, 16 + hp:16 + hp + 1]: e.tensor_scalar(
                                 vp_sb[:, qc * 512:(qc + 1) * 512],
                                 pq[pb][:, off:off + 512], bcol, None, ALU.add),
                             wait=[(sT, sT.n)], inc=sV)
                    qkv_evac.append(sV.n)
                    combo += 1
            wq_user[i] = sT.n
            for src, dst2 in ((qp_sb, mq2_sb), (kpz_sb, mk2_sb)):
                if src is kpz_sb:
                    P.op("vector", lambda e: e.tensor_mul(
                        sq2_sb[0:64, :], kpz_sb[0][0:64, :], kpz_sb[0][0:64, :]),
                        wait=[(sV, sV.n), (sT, sT.n)], inc=sV)
                    P.op("vector", lambda e: e.tensor_mul(
                        sq2_sb[64:128, :], kpz_sb[1][64:128, :],
                        kpz_sb[1][64:128, :]), inc=sV)
                else:
                    P.op("vector", lambda e, src=src: e.tensor_mul(
                        sq2_sb[:], src[:], src[:]),
                        wait=[(sV, sV.n), (sT, sT.n)], inc=sV)
                vv = sV.n
                for cc in range(4):
                    P.op("tensor", lambda e, cc=cc: e.matmul(
                        pq[cc // 2][0:1, (cc % 2) * 512:(cc % 2) * 512 + 512],
                        onesb[:, 0:1], sq2_sb[:, cc * 512:(cc + 1) * 512],
                        start=True, stop=True),
                        wait=[(sV, vv)] if cc == 0 else (),
                        inc=sT if cc == 3 else None)
                P.op("vector", lambda e, dst2=dst2: e.tensor_reduce(
                    dst2[0:1, 0:1], pq[0][0:1, :], AX.X, ALU.max),
                    wait=[(sT, sT.n)], inc=sV)
                P.op("vector", lambda e, dst2=dst2: e.tensor_reduce(
                    dst2[0:1, 1:2], pq[1][0:1, :], AX.X, ALU.max), inc=sV)
                vgap(2)
                P.op("vector", lambda e, dst2=dst2: e.tensor_tensor(
                    dst2[0:1, 0:1], dst2[0:1, 0:1], dst2[0:1, 1:2], ALU.max),
                    inc=sV)
                vgap(2)
            P.op("vector", lambda e: e.tensor_scalar(
                cc_sb[:], mq2_sb[0:1, 0:1], mk2_sb[0:1, 0:1], 0.5,
                ALU.add, ALU.mult), inc=sV)
            vgap(2)
            P.op("vector", lambda e: e.tensor_scalar(
                cc_sb[:], cc_sb[:], bmax_sb[0:1, 0:1], None, ALU.add), inc=sV)
            vgap(2)
            P.op("tensor", lambda e: e.matmul(pq[0][:, 0:1], onesf[0:1, 0:128],
                                              cc_sb[:], start=True, stop=True),
                 wait=[(sV, sV.n)], inc=sT)
            P.op("vector", lambda e: e.tensor_scalar_mul(
                negc_sb[:], pq[0][:, 0:1], -1.0), wait=[(sT, sT.n)], inc=sV)
            for hl in range(2):
                h = 2 * hp + hl
                P.op("vector", lambda e, hl=hl, h=h: e.tensor_sub(
                    t31mc[hl][:], t31c[:, h:h + 1], pq[0][:, 0:1]), inc=sV)
                P.op("vector", lambda e, hl=hl, h=h: e.tensor_sub(
                    t15mc[hl][:], t15c[:, h:h + 1], pq[0][:, 0:1]), inc=sV)
            caps_done = sV.n
            for hl in range(2):
                h = 2 * hp + hl
                P.op("sync", lambda e, hl=hl, h=h: e.dma_start(
                    band_sb[hl][:],
                    bass.AP(gscr.tensor, h * GLEN + (S - 1) - (BW + 127),
                            [[1, 128], [1, BANDW]])),
                    wait=[(sDx, gscr_done), (sS, sS.n)], inc=sDx, inc_n=16)
            band_done = sDx.n
            for hl in range(2):
                for kc in range(16):
                    P.op("tensor", lambda e, hl=hl, kc=kc: e.matmul(
                        pq[2 + hl][:, kc * 64:(kc + 1) * 64],
                        vp_sb[hl * 64:hl * 64 + 64, kc * 128:(kc + 1) * 128],
                        idb[hl * 64:hl * 64 + 64, hl * 64:hl * 64 + 64],
                        start=True, stop=True, tile_position=(hl * 64, 0)),
                        wait=[(sV, caps_done)] if kc == 0 else (),
                        inc=sT if kc == 15 else None)
            for hl in range(2):
                P.op("vector", lambda e, hl=hl: e.tensor_copy(
                    vaug_sb[hl][:].rearrange("p (k j) -> p k j", j=65)[:, :, 0:64],
                    pq[2 + hl][:].rearrange("p (k j) -> p k j", j=64)),
                    wait=[(sT, sT.n)], inc=sV)
            qh_gate = sV.n
            s_guard = [sS.n] * 4
            pt_guard = [[sT.n, sT.n], [sT.n, sT.n]]
            for qh in range(1 if stage in ("attn2", "attn4") else 2):
                q0 = qh * 1024
                pv_pend = []
                it = 0
                for kc in range(16):
                    for hl in range(2):
                        par = kc % 2
                        for ch in range(2):
                            slot = it % 4
                            spq, soff = pq[slot // 2], (slot % 2) * 512
                            qa = q0 + ch * 512
                            k0 = kc * 128
                            Lc = min(max(k0 - BW - qa, 0), 512)
                            Rc = min(max(k0 + BW + 128 - qa, 0), 512)
                            P.op("tensor", lambda e, hl=hl, kc=kc, qa=qa, spq=spq,
                                 soff=soff: e.matmul(
                                     spq[:, soff:soff + 512],
                                     kpz_sb[hl][:, kc * 128:(kc + 1) * 128],
                                     qp_sb[:, qa:qa + 512],
                                     start=True, stop=True),
                                 wait=[(sS, s_guard[slot]), (sV, qh_gate)],
                                 inc=sT)
                            qk_t = sT.n
                            bnd_v = None
                            if Rc > Lc:
                                jf = k0 + BW + 127 - (qa + Lc)
                                jl = jf - (Rc - Lc) + 1
                                stop = jl - 1 if jl > 0 else None
                                P.op("vector", lambda e, hl=hl, Lc=Lc, Rc=Rc,
                                     jf=jf, stop=stop, spq=spq, soff=soff:
                                     e.tensor_tensor(
                                         spq[:, soff + Lc:soff + Rc],
                                         spq[:, soff + Lc:soff + Rc],
                                         band_sb[hl][:, jf:stop:-1], ALU.add),
                                     wait=[(sT, qk_t), (sDx, band_done)], inc=sV)
                                bnd_v = sV.n
                            we = [(sT, qk_t), (sT, pt_guard[hl][par])]
                            if bnd_v is not None:
                                we.append((sV, bnd_v))
                            regs = []
                            if Lc > 0:
                                regs.append((0, Lc, t31mc[hl]))
                            if Rc > Lc:
                                regs.append((Lc, Rc, negc_sb))
                            if Rc < 512:
                                regs.append((Rc, 512, t15mc[hl]))
                            for ri, (a, bb, bias) in enumerate(regs):
                                P.op("scalar", lambda e, hl=hl, par=par, a=a,
                                     bb=bb, bias=bias, ch=ch, spq=spq, soff=soff:
                                     e.activation(
                                         pt_sb[hl][par][:, ch * 512 + a:
                                                        ch * 512 + bb],
                                         spq[:, soff + a:soff + bb],
                                         AF.Exp, bias=bias[:, 0:1], scale=1.0),
                                     wait=we if ri == 0 else (),
                                     inc=sS if ri == len(regs) - 1 else None)
                            s_guard[slot] = sS.n
                            pv_pend.append((kc, hl, par, ch, sS.n))
                            if len(pv_pend) > 2:
                                kc2, hl2, par2, ch2, g2 = pv_pend.pop(0)
                                P.op("tensor", lambda e, hl=hl2, kc=kc2, par=par2,
                                     ch=ch2: e.matmul(
                                         pq[2 + hl][0:65, ch * 512:(ch + 1) * 512],
                                         vaug_sb[hl][:, kc * 65:kc * 65 + 65],
                                         pt_sb[hl][par][:, ch * 512:(ch + 1) * 512],
                                         start=(kc == 0), stop=(kc == 15)),
                                     wait=[(sS, g2)], inc=sT)
                                pt_guard[hl2][par2] = sT.n
                            it += 1
                for kc2, hl2, par2, ch2, g2 in pv_pend:
                    P.op("tensor", lambda e, hl=hl2, kc=kc2, par=par2, ch=ch2:
                         e.matmul(pq[2 + hl][0:65, ch * 512:(ch + 1) * 512],
                                  vaug_sb[hl][:, kc * 65:kc * 65 + 65],
                                  pt_sb[hl][par][:, ch * 512:(ch + 1) * 512],
                                  start=(kc == 0), stop=(kc == 15)),
                         wait=[(sS, g2)], inc=sT)
                    pt_guard[hl2][par2] = sT.n
                pv_pend = []
                nhl = 1 if stage == "attn2" else 2
                ev1 = [0, 0]
                for hl in range(nhl):
                    P.op("vector", lambda e, hl=hl: e.tensor_copy(
                        ctxu2[hl][0:65, :], pq[2 + hl][0:65, :]),
                        wait=[(sT, pt_guard[hl][1]), (sT, pt_guard[hl][0])], inc=sV)
                    ev1[hl] = sV.n
                tzz = [0, 0]
                for hl in range(nhl):
                    for ch in range(8):
                        P.op("tensor", lambda e, ch=ch, hl=hl: e.matmul(
                            pq[hl][:, 512 + ch:512 + ch + 1],
                            ctxu2[hl][64:65, ch * 128:(ch + 1) * 128],
                            onesb[64:65, 0:1], start=True, stop=True),
                            wait=[(sV, ev1[hl])] if ch == 0 else (),
                            inc=sT if ch == 7 else None)
                    tzz[hl] = sT.n
                for hl in range(nhl):
                    P.op("vector", lambda e, hl=hl: e.reciprocal(
                        rz2[hl][:], pq[hl][:, 512:520]),
                        wait=[(sT, tzz[hl])], inc=sV)
                tz2 = [0, 0]
                for hl in range(nhl):
                    for ch in range(8):
                        P.op("tensor", lambda e, ch=ch, hl=hl: e.matmul(
                            pq[hl][:, ch * 64:(ch + 1) * 64],
                            ctxu2[hl][0:64, ch * 128:(ch + 1) * 128],
                            idb[0:64, 0:64], start=True, stop=True),
                            inc=sT if ch == 7 else None)
                    tz2[hl] = sT.n
                ev2 = [0, 0]
                for hl in range(nhl):
                    P.op("vector", lambda e, hl=hl: e.tensor_tensor(
                        ctxtm2[hl][:].rearrange("p (c j) -> p c j", j=64),
                        pq[hl][:, 0:512].rearrange("p (c j) -> p c j", j=64),
                        rz2[hl][:].rearrange("p (c o) -> p c o", o=1)
                        .broadcast_to([128, 8, 64]), ALU.mult),
                        wait=[(sT, tz2[hl])], inc=sV)
                    ev2[hl] = sV.n
                for hl in range(nhl):
                    for ch in range(8):
                        P.op("tensor", lambda e, ch=ch, hl=hl: e.matmul(
                            pq[hl][0:64, ch * 128:(ch + 1) * 128],
                            ctxtm2[hl][:, ch * 64:(ch + 1) * 64], idb[:],
                            start=True, stop=True),
                            wait=[(sV, ev2[hl])] if ch == 0 else (),
                            inc=sT if ch == 7 else None)
                    tf = sT.n
                    P.op("vector", lambda e, hl=hl, hp=hp, q0=q0: e.tensor_copy(
                        ctx_sb[hl * 64:hl * 64 + 64, hp, q0:q0 + 1024],
                        pq[hl][0:64, :]),
                        wait=[(sT, tf)], inc=sV)
                qh_gate = sV.n
                s_guard = [sS.n] * 4
                pt_guard = [[sT.n, sT.n], [sT.n, sT.n]]
        if stage == "attn":
            dump("ctx", ctx_sb[:], [(sV, sV.n)])
            break
        if stage == "attn4":
            dump("t2", sq2_sb[:, 0:512], [(sV, sV.n)])
            dump("rz", rz2[1][:], [(sV, sV.n)])
            dump("ctxtm", ctxtm2[1][:], [(sV, sV.n)])
            dump("ctxu", ctxu2[1][:], [(sV, sV.n)])
            dump("ctx", ctx_sb[:], [(sV, sV.n)])
            dump("qp", qp_sb[:], [(sV, sV.n)])
            dump("kp", kpz_sb[0][:], [(sV, sV.n)])
            dump("vp", vp_sb[:], [(sV, sV.n)])
            dump("negc", negc_sb[:], [(sV, sV.n)])
            break
        if stage == "attn2":
            dump("mq2", mq2_sb[:], [(sV, sV.n)])
            dump("mk2", mk2_sb[:], [(sV, sV.n)])
            dump("ccv", cc_sb[:], [(sV, sV.n)])
            dump("bmax", bmax_sb[:], [(sV, sV.n)])
            dump("brow", brow_sb[:], [(sV, sV.n)])
            dump("t31c", t31c[:], [(sV, sV.n)])
            dump("t15c", t15c[:], [(sV, sV.n)])
            dump("t31mc0", t31mc[0][:], [(sV, sV.n)])
            dump("t15mc0", t15mc[0][:], [(sV, sV.n)])
            dump("t31mc1", t31mc[1][:], [(sV, sV.n)])
            dump("sq2", sq2_sb[:], [(sV, sV.n)])
            dump("ctxu", ctxu2[0][:], [(sV, sV.n)])
            dump("ctx", ctx_sb[:], [(sV, sV.n)])
            dump("pt00", pt_sb[0][0][:], [(sV, sV.n)])
            dump("pt01", pt_sb[0][1][:], [(sV, sV.n)])
            dump("qp", qp_sb[:], [(sV, sV.n)])
            dump("kp", kpz_sb[0][:], [(sV, sV.n)])
            dump("vp", vp_sb[:], [(sV, sV.n)])
            dump("negc", negc_sb[:], [(sV, sV.n)])
            break
        if stage == "attn1":
            dump("ctx", ctx_sb[:], [(sV, sV.n)])
            dump("qp", qp_sb[:], [(sV, sV.n)])
            dump("kp", kpz_sb[0][:], [(sV, sV.n)])
            dump("vp", vp_sb[:], [(sV, sV.n)])
            dump("vaug0", vaug_sb[0][:], [(sV, sV.n)])
            dump("pt00", pt_sb[0][0][:], [(sV, sV.n)])
            dump("pt01", pt_sb[0][1][:], [(sV, sV.n)])
            dump("pt10", pt_sb[1][0][:], [(sV, sV.n)])
            dump("negc", negc_sb[:], [(sV, sV.n)])
            dump("t31mc0", t31mc[0][:], [(sV, sV.n)])
            dump("t15mc0", t15mc[0][:], [(sV, sV.n)])
            dump("band0", band_sb[0][:], [(sV, sV.n)])
            break

        P.op("gpsimd", lambda e: e.dma_start(
            wdf_sb[:], wd_d.rearrange("(pr p) d -> p pr d", p=128)),
            wait=[(sT, sT.n)], inc=sDw, inc_n=16)
        wd_ready = sDw.n
        prj_v = sV.n
        for qc in range(4):
            for pair in range(NPAIR):
                for do in range(NDC):
                    P.op("tensor", lambda e, pair=pair, do=do, qc=qc: e.matmul(
                        pq[do // 2][:, (do % 2) * 512:(do % 2) * 512 + 512],
                        wdf_sb[:, pair, do * 128:(do + 1) * 128],
                        ctx_sb[:, pair, qc * 512:(qc + 1) * 512],
                        start=(pair == 0), stop=(pair == NPAIR - 1)),
                        wait=[(sDw, wd_ready), (sV, prj_v)]
                        if (pair == 0 and do == 0) else (),
                        inc=sT if (pair == NPAIR - 1 and do == NDC - 1) else None)
            tprj = sT.n
            for do in range(NDC):
                P.op("vector", lambda e, do=do, qc=qc: e.scalar_tensor_tensor(
                    x_sb[:, do, qc * 512:(qc + 1) * 512],
                    pq[do // 2][:, (do % 2) * 512:(do % 2) * 512 + 512],
                    bd_sb[:, do:do + 1], x_sb[:, do, qc * 512:(qc + 1) * 512],
                    ALU.add, ALU.add),
                    wait=[(sT, tprj)], inc=sV)
            prj_v = sV.n
        if stage == "proj":
            dump("out1", x_sb[:], [(sV, sV.n)])
            break

        emit_ln()
        if stage == "ln2":
            dump("u2", u_sb[:], [(sV, sV.n)])
            break

        lg0 = sV.n
        for qh2 in range(2):
            for cc in range(2):
                for dc in range(NDC):
                    P.op("tensor", lambda e, qh2=qh2, cc=cc, dc=dc: e.matmul(
                        pq[qh2][0:E, cc * 512:(cc + 1) * 512],
                        wg_sb[:, dc * E:(dc + 1) * E],
                        u_sb[:, dc, qh2 * 1024 + cc * 512:qh2 * 1024 + (cc + 1) * 512],
                        start=(dc == 0), stop=(dc == NDC - 1)),
                        wait=[(sV, lg0), (sDw, 16)] if dc == 0 else (),
                        inc=sT if (dc == NDC - 1 and cc == 1) else None)
        for qh2 in range(2):
            P.op("vector", lambda e, qh2=qh2: e.tensor_copy(
                logits_sb[:, qh2 * 1024:(qh2 + 1) * 1024], pq[qh2][0:E, :]),
                wait=[(sT, sT.n)], inc=sV)
        lge = sV.n
        for ch in range(16):
            P.op("tensor", lambda e, ch=ch: e.matmul(
                pq[2][:, ch * 8:(ch + 1) * 8],
                logits_sb[:, ch * 128:(ch + 1) * 128], idf[0:E, 0:E],
                start=True, stop=True),
                wait=[(sV, lge)] if ch == 0 else (), inc=sT if ch == 15 else None)
        P.op("vector", lambda e: e.tensor_copy(lg_tm[:], pq[2][:, 0:128]),
             wait=[(sT, sT.n)], inc=sV)
        vgap(2)
        lg3 = lg_tm[:].rearrange("p (c j) -> p c j", j=8)
        m13 = m1_sb[:].rearrange("p (c o) -> p c o", o=1)
        m23 = m2_sb[:].rearrange("p (c o) -> p c o", o=1)
        pex3 = pexp_sb[:].rearrange("p (c j) -> p c j", j=8)
        P.op("vector", lambda e: e.tensor_reduce(m13, lg3, AX.X, ALU.max), inc=sV)
        vgap(2)
        P.op("vector", lambda e: e.tensor_tensor(
            pex3, lg3, m13.broadcast_to([128, 16, 8]), ALU.subtract), inc=sV)
        P.op("scalar", lambda e: e.activation(pexp_sb[:], pexp_sb[:], AF.Exp),
             wait=[(sV, sV.n)], inc=sS)
        P.op("vector", lambda e: e.tensor_reduce(m13, pex3, AX.X, ALU.max),
             wait=[(sS, sS.n)], inc=sV)
        vgap(2)
        P.op("vector", lambda e: e.tensor_tensor(
            keep_sb[:].rearrange("p (c j) -> p c j", j=8), pex3,
            m13.broadcast_to([128, 16, 8]), ALU.is_ge), inc=sV)
        vgap(2)
        P.op("vector", lambda e: e.scalar_tensor_tensor(
            q2_sb[:], keep_sb[:], -1e30, pexp_sb[:], ALU.mult, ALU.add), inc=sV)
        vgap(2)
        P.op("vector", lambda e: e.tensor_reduce(
            m23, q2_sb[:].rearrange("p (c j) -> p c j", j=8), AX.X, ALU.max),
            inc=sV)
        vgap(2)
        P.op("vector", lambda e: e.tensor_tensor(
            keep_sb[:].rearrange("p (c j) -> p c j", j=8), pex3,
            m23.broadcast_to([128, 16, 8]), ALU.is_ge), inc=sV)
        vgap(2)
        P.op("vector", lambda e: e.tensor_mul(q2_sb[:], pexp_sb[:], keep_sb[:]),
             inc=sV)
        P.op("vector", lambda e: e.tensor_reduce(
            den_sb[:].rearrange("p (c o) -> p c o", o=1), pex3, AX.X, ALU.add),
            inc=sV)
        vgap(2)
        P.op("vector", lambda e: e.scalar_tensor_tensor(
            den_sb[:], den_sb[:], 1e-8, m2_sb[:], ALU.mult, ALU.add), inc=sV)
        vgap(2)
        P.op("vector", lambda e: e.tensor_add(den_sb[:], den_sb[:], m1_sb[:]),
             inc=sV)
        vgap(2)
        P.op("vector", lambda e: e.reciprocal(rden_sb[:], den_sb[:]), inc=sV)
        vgap(2)
        P.op("vector", lambda e: e.tensor_tensor(
            cval_sb[:].rearrange("p (c j) -> p c j", j=8),
            q2_sb[:].rearrange("p (c j) -> p c j", j=8),
            rden_sb[:].rearrange("p (c o) -> p c o", o=1).broadcast_to([128, 16, 8]),
            ALU.mult), inc=sV)
        P.op("scalar", lambda e: e.activation(sqctm_sb[:], cval_sb[:], AF.Sqrt),
             wait=[(sV, sV.n)], inc=sS)
        sqv = sS.n
        for ch in range(16):
            pb = 2 + (ch // 8)
            P.op("tensor", lambda e, ch=ch, pb=pb: e.matmul(
                pq[pb][0:E, (ch % 8) * 128:(ch % 8 + 1) * 128],
                sqctm_sb[:, ch * 8:(ch + 1) * 8], idb[:],
                start=True, stop=True),
                wait=[(sS, sqv)] if ch == 0 else (),
                inc=sT if ch in (7, 15) else None)
        for half in range(2):
            P.op("vector", lambda e, half=half: e.tensor_copy(
                sqce_sb[:, half * 1024:(half + 1) * 1024], pq[2 + half][0:E, :]),
                wait=[(sT, sT.n)], inc=sV)
        if stage == "gate":
            dump("sqce", sqce_sb[:], [(sV, sV.n)])
            break

        def w1_dma(e_idx, wait_t):
            P.op("gpsimd", lambda e, e_idx=e_idx: e.dma_start(
                w1sl[:], w1_d[e_idx].rearrange("(dc p) h -> p dc h", p=128)),
                wait=[(sT, wait_t)], inc=sDw, inc_n=16)
            return sDw.n

        def w2_dma(e_idx, wait_t):
            P.op("gpsimd", lambda e, e_idx=e_idx: e.dma_start(
                w2sl[:], w2_d[e_idx].rearrange("(jc p) d -> p jc d", p=128)),
                wait=[(sT, wait_t)], inc=sDw, inc_n=16)
            return sDw.n

        w1_ready = w1_dma(0, sT.n)
        w2_ready = w2_dma(0, sT.n)
        for ei in range(E):
            sq0 = sV.n
            for qc in range(4):
                P.op("tensor", lambda e, ei=ei, qc=qc: e.matmul(
                    pq[qc // 2][:, (qc % 2) * 512:(qc % 2) * 512 + 512],
                    sel8[:, ei * 128:(ei + 1) * 128],
                    sqce_sb[:, qc * 512:(qc + 1) * 512], start=True, stop=True),
                    wait=[(sV, sq0)] if qc == 0 else (),
                    inc=sT if qc == 3 else None)
            for half in range(2):
                P.op("vector", lambda e, half=half: e.tensor_copy(
                    sqcb_sb[:, half * 1024:(half + 1) * 1024], pq[half][:, 0:1024]),
                    wait=[(sT, sT.n)], inc=sV)
            sqcb_done = sV.n
            for qc in range(4):
                ue_t = sT.n
                for dc in range(NDC):
                    P.op("vector", lambda e, dc=dc, qc=qc: e.tensor_mul(
                        ue_sb[:, dc, :], u_sb[:, dc, qc * 512:(qc + 1) * 512],
                        sqcb_sb[:, qc * 512:(qc + 1) * 512]),
                        wait=[(sV, sqcb_done), (sT, ue_t)] if dc == 0 else (),
                        inc=sV)
                ue_done = sV.n
                relu_v = {}
                g_v = {}
                for hc in range(NHC):
                    slot = hc % 4
                    pb, off = slot // 2, (slot % 2) * 512
                    if hc < 4:
                        w = [(sDw, w1_ready), (sV, ue_done)]
                    else:
                        w = [(sV, relu_v[hc - 4])]
                    for dc in range(NDC):
                        P.op("tensor", lambda e, hc=hc, dc=dc, pb=pb, off=off:
                             e.matmul(pq[pb][:, off:off + 512],
                                      w1sl[:, dc, hc * 128:(hc + 1) * 128],
                                      ue_sb[:, dc, :],
                                      start=(dc == 0), stop=(dc == NDC - 1)),
                             wait=w if dc == 0 else (),
                             inc=sT if dc == NDC - 1 else None)
                    P.op("vector", lambda e, pb=pb, off=off, hc=hc:
                         e.tensor_scalar_max(
                             trelu_sb2[hc % 2][:], pq[pb][:, off:off + 512], 0.0),
                         wait=[(sT, sT.n)], inc=sV)
                    relu_v[hc] = sV.n
                    if hc > 0:
                        P.op("vector", lambda e, hc=hc: e.tensor_mul(
                            gg_sb[:, hc - 1, :], trelu_sb2[(hc - 1) % 2][:],
                            trelu_sb2[(hc - 1) % 2][:]), inc=sV)
                        g_v[hc - 1] = sV.n
                P.op("vector", lambda e: e.tensor_mul(
                    gg_sb[:, NHC - 1, :], trelu_sb2[(NHC - 1) % 2][:],
                    trelu_sb2[(NHC - 1) % 2][:]), inc=sV)
                g_v[NHC - 1] = sV.n
                w1_user = sT.n
                g_done = sV.n
                if ei + 1 < E and qc == 3:
                    w1_ready = w1_dma(ei + 1, w1_user)
                for do in list(range(4, NDC)) + list(range(4)):
                    for jc in range(NHC):
                        w = []
                        if do == 4:
                            w = [(sDw, w2_ready), (sV, g_v[jc])]
                        elif do == 0 and jc == 0:
                            w = [(sV, g_done)]
                        P.op("tensor", lambda e, do=do, jc=jc: e.matmul(
                            pq[do // 2][:, (do % 2) * 512:(do % 2) * 512 + 512],
                            w2sl[:, jc, do * 128:(do + 1) * 128],
                            gg_sb[:, jc, :], start=(jc == 0), stop=(jc == NHC - 1)),
                            wait=w,
                            inc=sT if jc == NHC - 1 else None)
                w2_user = sT.n
                ty = sT.n
                for do in range(NDC):
                    P.op("vector", lambda e, do=do, qc=qc: e.tensor_add(
                        x_sb[:, do, qc * 512:(qc + 1) * 512],
                        x_sb[:, do, qc * 512:(qc + 1) * 512],
                        pq[do // 2][:, (do % 2) * 512:(do % 2) * 512 + 512]),
                        wait=[(sT, ty)], inc=sV)
                if ei + 1 < E and qc == 3:
                    w2_ready = w2_dma(ei + 1, w2_user)
        if stage == "ffn":
            dump("out2", x_sb[:], [(sV, sV.n)])
            break

        for qch in range(16):
            i = qch % 2
            for grp in range(2):
                for s4 in range(4):
                    dsl = grp * 4 + s4
                    P.op("tensor", lambda e, dsl=dsl, qch=qch, grp=grp, s4=s4:
                         e.matmul(pq[grp][:, s4 * 128:(s4 + 1) * 128],
                                  x_sb[:, dsl, qch * 128:(qch + 1) * 128], idf[:],
                                  start=True, stop=True),
                         wait=[(sV, sV.n)] if (grp == 0 and s4 == 0) else (),
                         inc=sT if s4 == 3 else None)
            tr1 = sT.n
            P.op("vector", lambda e, i=i: e.tensor_copy(
                ot_sb[i][:, 0:512], pq[0][:, 0:512]),
                wait=[(sT, tr1), (sDx, ot_last_store[i])], inc=sV)
            P.op("vector", lambda e, i=i: e.tensor_copy(
                ot_sb[i][:, 512:1024], pq[1][:, 0:512]), inc=sV)
            P.op("sync", lambda e, b=b, qch=qch, i=i: e.dma_start(
                y_d[b, qch * 128:(qch + 1) * 128, :], ot_sb[i][:]),
                wait=[(sV, sV.n)], inc=sDx, inc_n=16)
            ot_last_store[i] = sDx.n

    P.op("sync", lambda e: e.nop(), wait=[(sDx, sDx.n), (sDw, sDw.n)])

    with nc.Block() as blk:
        blk.tensor(P.emitter("tensor"))
        blk.vector(P.emitter("vector"))
        blk.scalar(P.emitter("scalar"))
        blk.sync(P.emitter("sync"))
        blk.gpsimd(P.emitter("gpsimd"))
    return nc, dbg


def _consts():
    bk = _bucket_of_rel()
    oh = np.zeros((NB, GLEN), np.float32)
    oh[bk, np.arange(GLEN)] = 1.0
    sel8 = np.zeros((E, E * 128), ml_dtypes.bfloat16)
    for e in range(E):
        sel8[e, e * 128:(e + 1) * 128] = 1.0
    return {"oh": oh,
            "cones": np.ones((128, 128), ml_dtypes.bfloat16),
            "cidb": np.eye(128, dtype=ml_dtypes.bfloat16),
            "cidf": np.eye(128, dtype=np.float32),
            "conesf": np.ones((128, 128), np.float32),
            "csel8": sel8}


_NC_CACHE = {}


def kernel(L0_attn_hidden_states, L0_attn_w_qkv, L0_attn_b_qkv, L0_attn_w_dense,
           L0_attn_b_dense, L0_attn_rel_attn_table, L0_ffn_Wg, L0_ffn_W1,
           L0_ffn_W2):
    x = np.ascontiguousarray(L0_attn_hidden_states, dtype=np.float32)
    if "full" not in _NC_CACHE:
        _NC_CACHE["full"] = build_nc("full")
    nc, _ = _NC_CACHE["full"]
    shared = _consts()
    shared.update({
        "wqkv": np.ascontiguousarray(L0_attn_w_qkv, np.float32),
        "bqkv": np.ascontiguousarray(L0_attn_b_qkv, np.float32),
        "wdense": np.ascontiguousarray(L0_attn_w_dense, np.float32),
        "bdense": np.ascontiguousarray(L0_attn_b_dense, np.float32),
        "table": np.ascontiguousarray(L0_attn_rel_attn_table, np.float32),
        "wg": np.ascontiguousarray(L0_ffn_Wg, np.float32),
        "w1": np.ascontiguousarray(L0_ffn_W1, np.float32),
        "w2": np.ascontiguousarray(L0_ffn_W2, np.float32),
    })
    in_maps = []
    for c in range(NCORES):
        m = dict(shared)
        m["x"] = np.ascontiguousarray(x[c * BLOC:(c + 1) * BLOC])
        in_maps.append(m)
    res = run_bass_kernel_spmd(nc, in_maps, core_ids=list(range(NCORES)))
    return np.concatenate([r["y"] for r in res.results], axis=0)


# version 08
# speedup vs baseline: 1.0450x; 1.0450x over previous
import math

import ml_dtypes
import numpy as np

import concourse.bass as bass
import concourse.mybir as mybir
from concourse.bass_utils import run_bass_kernel_spmd

F32 = mybir.dt.float32
BF16 = mybir.dt.bfloat16
AF = mybir.ActivationFunctionType
ALU = mybir.AluOpType
AX = mybir.AxisListType

NCORES = 8
BLOC = 2
S = 2048
D = 1024
NDC = D // 128
NH = 16
HD = 64
NPAIR = NH // 2
E = 8
H = 2048
NHC = H // 128
NB = 32
MAXD = 128
ISQ8 = float(1.0 / math.sqrt(8.0))
EPS_LN = 1e-5
BW = 90
BANDW = 2 * BW + 128
GLEN = 2 * S - 1


def _bucket_of_rel():
    rel = np.arange(-(S - 1), S, dtype=np.int64)
    nb = NB // 2
    b = np.where(rel > 0, nb, 0)
    ar = np.abs(rel)
    me = nb // 2
    small = ar < me
    tmp = np.log(np.maximum(ar, 1).astype(np.float32) / np.float32(me))
    large = (tmp / np.float32(math.log(MAXD / me)) * np.float32(nb - me)).astype(np.int32)
    large = me + large
    large = np.minimum(large, nb - 1)
    return (b + np.where(small, ar, large)).astype(np.int32)


class Sem:
    __slots__ = ("h", "n")

    def __init__(self, h):
        self.h = h
        self.n = 0


class Plan:
    ENGS = ("tensor", "vector", "scalar", "sync", "gpsimd")

    def __init__(self):
        self.ops = {e: [] for e in self.ENGS}

    def op(self, eng, fn, wait=(), inc=None, inc_n=1):
        for s, v in wait:
            assert v <= s.n, f"deadlock: wait {v} > planned {s.n} on {eng}"
        self.ops[eng].append((tuple((s.h, v) for s, v in wait if v > 0), fn,
                              inc.h if inc is not None else None, inc_n))
        if inc is not None:
            inc.n += inc_n
        return inc.n if inc is not None else None

    def emitter(self, eng_name):
        ops = self.ops[eng_name]

        def emit(eng):
            waited = {}
            for waits, fn, inch, incn in ops:
                for h, v in waits:
                    if waited.get(id(h), -1) < v:
                        eng.wait_ge(h, v)
                        waited[id(h)] = v
                inst = fn(eng)
                if inch is not None:
                    inst.then_inc(inch, incn)
        return emit


def build_nc(stage="full"):
    nc = bass.Bass("TRN2", target_bir_lowering=False,
                   detect_race_conditions=False)

    x_d = nc.dram_tensor("x", [BLOC, S, D], F32, kind="ExternalInput").ap()
    wqkv_d = nc.dram_tensor("wqkv", [D, 3 * D], F32, kind="ExternalInput").ap()
    bqkv_d = nc.dram_tensor("bqkv", [3 * D], F32, kind="ExternalInput").ap()
    wd_d = nc.dram_tensor("wdense", [D, D], F32, kind="ExternalInput").ap()
    bd_d = nc.dram_tensor("bdense", [D], F32, kind="ExternalInput").ap()
    tbl_d = nc.dram_tensor("table", [NH, NB], F32, kind="ExternalInput").ap()
    wg_d = nc.dram_tensor("wg", [D, E], F32, kind="ExternalInput").ap()
    w1_d = nc.dram_tensor("w1", [E, D, H], F32, kind="ExternalInput").ap()
    w2_d = nc.dram_tensor("w2", [E, H, D], F32, kind="ExternalInput").ap()
    oh_d = nc.dram_tensor("oh", [NB, GLEN], F32, kind="ExternalInput").ap()
    ones_d = nc.dram_tensor("cones", [128, 128], BF16, kind="ExternalInput").ap()
    idb_d = nc.dram_tensor("cidb", [128, 128], BF16, kind="ExternalInput").ap()
    idf_d = nc.dram_tensor("cidf", [128, 128], F32, kind="ExternalInput").ap()
    onesf_d = nc.dram_tensor("conesf", [128, 128], F32, kind="ExternalInput").ap()
    sel8_d = nc.dram_tensor("csel8", [E, E * 128], BF16, kind="ExternalInput").ap()
    y_d = nc.dram_tensor("y", [BLOC, S, D], F32, kind="ExternalOutput").ap()
    gscr = nc.dram_tensor("gscr", [NH, GLEN], F32, kind="Internal").ap()

    def alloc(name, shape, dtype, off):
        assert off % 32 == 0, (name, off)
        return nc.alloc_sbuf_tensor_at(name, list(shape), dtype, offset=off).ap()

    def aup(v):
        return (v + 31) & ~31

    BASE = 17024
    R_X, R_U, R_C, R_D = BASE, BASE + 65536, BASE + 98304, BASE + 131072
    R_E, R_F = BASE + 163840, BASE + 180224
    R_G, R_H = BASE + 192512, BASE + 200704
    x_sb = alloc("x_sb", [128, NDC, S], F32, R_X)
    u_sb = alloc("u_sb", [128, NDC, S], BF16, R_U)
    ctx_sb = alloc("ctx_sb", [128, NPAIR, S], BF16, R_C)
    w1sl = alloc("w1sl", [128, NDC, H], BF16, R_C)
    oh_sb = alloc("oh_sb", [NB, GLEN], F32, R_C)
    g_sbuf = alloc("g_sbuf", [NH, GLEN], F32, R_D)
    w2sl = alloc("w2sl", [128, NHC, D], BF16, R_D)
    wdf_sb = alloc("wdf_sb", [128, NPAIR, D], BF16, R_D)
    dD = R_D
    qp_sb = alloc("qp_sb", [128, S], BF16, dD); dD += 4096
    kpz_sb = []
    for hl in range(2):
        kpz_sb.append(alloc(f"kpz{hl}", [128, S], BF16, dD)); dD += 4096
    vp_sb = alloc("vp_sb", [128, S], BF16, dD); dD += 4096
    pt_sb = [[None, None], [None, None]]
    for hl in range(2):
        for par in range(2):
            pt_sb[hl][par] = alloc(f"pt{hl}{par}", [128, 1024], BF16, dD); dD += 2048
    vaug_sb = []
    for hl in range(2):
        vaug_sb.append(alloc(f"vaug{hl}", [128, 16 * 65], BF16, dD)); dD += 2176
    band_sb = []
    for hl in range(2):
        band_sb.append(alloc(f"band{hl}", [128, BANDW], F32, dD)); dD += 1280
    ctxu2 = [alloc("ctxu0", [128, 1024], BF16, R_G + 4096),
             alloc("ctxu1", [128, 1024], BF16, R_D + 16384)]
    ctxtm2 = [alloc("ctxtm0", [128, 512], BF16, R_G + 4096 + 2048),
              alloc("ctxtm1", [128, 512], BF16, R_D + 16384 + 2048)]
    assert dD - R_D <= 32768
    dL = R_D
    m_row = alloc("m_row", [1, 1024], F32, dL); dL += 4096
    v_row = alloc("v_row", [1, 1024], F32, dL); dL += 4096
    t_row = alloc("t_row", [1, 1024], F32, dL); dL += 4096
    sq_sb = alloc("sq_sb", [128, S], BF16, dL); dL += 4096
    logits_sb = alloc("logits_sb", [E, S], F32, dL); dL += 8192
    assert dL - R_D <= 32768
    gg_sb = alloc("gg_sb", [128, NHC, 512], BF16, R_E)
    dE = R_E
    lg_tm = alloc("lg_tm", [128, 128], F32, dE); dE += 512
    pexp_sb = alloc("pexp_sb", [128, 128], F32, dE); dE += 512
    m1_sb = alloc("m1_sb", [128, 16], F32, dE); dE += 64
    m2_sb = alloc("m2_sb", [128, 16], F32, dE); dE += 64
    den_sb = alloc("den_sb", [128, 16], F32, dE); dE += 64
    rden_sb = alloc("rden_sb", [128, 16], F32, dE); dE += 64
    keep_sb = alloc("keep_sb", [128, 128], F32, dE); dE += 512
    q2_sb = alloc("q2_sb", [128, 128], F32, dE); dE += 512
    cval_sb = alloc("cval_sb", [128, 128], F32, dE); dE += 512
    sqctm_sb = alloc("sqctm_sb", [128, 128], BF16, dE); dE += 256
    rz2 = [alloc(f"rz{i}", [128, 8], F32, dE + 32 * i) for i in range(2)]
    dE += 64
    assert dE - R_E <= 16384
    wqkv_sb = [alloc(f"wqkv{i}", [128, NDC, 3, 128], BF16, R_F + i * 6144)
               for i in range(2)]
    sqcb_sb = alloc("sqcb_sb", [128, S], BF16, R_F)
    sqce_sb = alloc("sqce_sb", [E, S], BF16, R_F + 4096)
    trelu_sb2 = [alloc(f"trelu{i}", [128, 512], BF16, R_F + 8192 + i * 1024)
                 for i in range(2)]
    xtm_sb = [alloc(f"xtm{i}", [128, 1024], F32, R_G + i * 4096) for i in range(2)]
    ot_sb = [alloc(f"ot{i}", [128, 1024], F32, R_G + i * 4096) for i in range(2)]
    sq2_sb = alloc("sq2_sb", [128, S], BF16, R_G + 4096)
    ue_sb = alloc("ue_sb", [128, NDC, 512], BF16, R_G)
    dH = R_H
    onesb = alloc("onesb", [128, 128], BF16, dH); dH = (dH + 256 + 31) & ~31
    idb = alloc("idb", [128, 128], BF16, dH); dH = (dH + 256 + 31) & ~31
    idf = alloc("idf", [128, 128], F32, dH); dH = (dH + 512 + 31) & ~31
    onesf = alloc("onesf", [128, 128], F32, dH); dH = (dH + 512 + 31) & ~31
    sel8 = alloc("sel8", [E, E * 128], BF16, dH); dH = (dH + 2048 + 31) & ~31
    bqkv_sb = alloc("bqkv_sb", [128, 24], F32, dH); dH = (dH + 96 + 31) & ~31
    bqs_sb = alloc("bqs_sb", [128, 16], F32, dH); dH = (dH + 64 + 31) & ~31
    bd_sb = alloc("bd_sb", [128, NDC], F32, dH); dH = (dH + 32 + 31) & ~31
    wg_sb = alloc("wg_sb", [128, NDC * E], BF16, dH); dH = (dH + 128 + 31) & ~31
    tbl_sb = alloc("tbl_sb", [NH, NB], F32, dH); dH = (dH + 128 + 31) & ~31
    tblT_sb = alloc("tblT_sb", [NB, NH], F32, dH); dH = (dH + 64 + 31) & ~31
    t15c = alloc("t15c", [128, NH], F32, dH); dH = (dH + 64 + 31) & ~31
    t31c = alloc("t31c", [128, NH], F32, dH); dH = (dH + 64 + 31) & ~31
    trow_sb = alloc("trow_sb", [1, 2 * NH], F32, dH); dH = (dH + 128 + 31) & ~31
    abst_sb = alloc("abst_sb", [NH, NB], F32, dH); dH = (dH + 128 + 31) & ~31
    brow_sb = alloc("brow_sb", [1, NB], F32, dH); dH = (dH + 128 + 31) & ~31
    bmax_sb = alloc("bmax_sb", [1, 1], F32, dH); dH = (dH + 4 + 31) & ~31
    mq2_sb = alloc("mq2_sb", [1, 2], F32, dH); dH = (dH + 8 + 31) & ~31
    mk2_sb = alloc("mk2_sb", [1, 2], F32, dH); dH = (dH + 8 + 31) & ~31
    cc_sb = alloc("cc_sb", [1, 1], F32, dH); dH = (dH + 4 + 31) & ~31
    negc_sb = alloc("negc_sb", [128, 1], F32, dH); dH = (dH + 4 + 31) & ~31
    t31mc = []
    t15mc = []
    for i in range(2):
        t31mc.append(alloc(f"t31mc{i}", [128, 1], F32, dH)); dH = (dH + 4 + 31) & ~31
        t15mc.append(alloc(f"t15mc{i}", [128, 1], F32, dH)); dH = (dH + 4 + 31) & ~31
    assert dH <= 223300, dH

    pq = [nc.alloc_psum_tensor(f"pq{i}", [128, 1024], F32).ap() for i in range(4)]

    fill_sb = alloc("fill_sb", [128, 8], F32, ((dH + 31) & ~31))
    P = Plan()
    sT = Sem(nc.alloc_semaphore("sT"))
    sV = Sem(nc.alloc_semaphore("sV"))
    sS = Sem(nc.alloc_semaphore("sS"))
    sDx = Sem(nc.alloc_semaphore("sDx"))
    sDw = Sem(nc.alloc_semaphore("sDw"))
    dbg = []

    def vgap(n=1):
        for _ in range(n):
            P.op("vector", lambda e: e.memset(fill_sb[:, 0:1], 0.0), inc=sV)

    def dump(name, ap, wait):
        d = nc.dram_tensor("dbg_" + name, list(ap.shape), ap.dtype,
                           kind="ExternalOutput").ap()
        P.op("sync", lambda e, d=d, a=ap: e.dma_start(d, a), wait=wait,
             inc=sDx, inc_n=16)
        dbg.append("dbg_" + name)

    for dst, src in ((onesb, ones_d), (idb, idb_d), (idf, idf_d), (onesf, onesf_d),
                     (sel8, sel8_d), (tbl_sb, tbl_d), (oh_sb, oh_d)):
        P.op("sync", lambda e, d=dst, s=src: e.dma_start(d, s), inc=sDx, inc_n=16)
    P.op("sync", lambda e: e.dma_start(
        bqkv_sb[:], bqkv_d.rearrange("(j p) -> p j", p=128),
        allow_slow_non_contiguous=True), inc=sDx, inc_n=16)
    P.op("sync", lambda e: e.dma_start(
        bd_sb[:], bd_d.rearrange("(j p) -> p j", p=128),
        allow_slow_non_contiguous=True), inc=sDx, inc_n=16)
    P.op("gpsimd", lambda e: e.dma_start(
        wg_sb[:].rearrange("p (j e) -> p j e", e=E),
        wg_d.rearrange("(j p) e -> p j e", p=128)), inc=sDw, inc_n=16)
    pre_ld = sDx.n
    P.op("tensor", lambda e: e.matmul(pq[0][0:NB, 0:NH], tbl_sb[:], idf[0:NH, 0:NH],
                                      start=True, stop=True),
         wait=[(sDx, pre_ld)], inc=sT)
    P.op("vector", lambda e: e.tensor_copy(tblT_sb[:], pq[0][0:NB, 0:NH]),
         wait=[(sT, sT.n)], inc=sV)
    tv0 = sV.n
    for ch in range(8):
        c0, c1 = ch * 512, min(GLEN, ch * 512 + 512)
        P.op("tensor", lambda e, c0=c0, c1=c1, b2=ch % 2: e.matmul(
            pq[2 + b2][0:NH, 0:c1 - c0], tblT_sb[:], oh_sb[:, c0:c1],
            start=True, stop=True), wait=[(sV, tv0 + max(0, ch - 1))], inc=sT)
        P.op("vector", lambda e, c0=c0, c1=c1, b2=ch % 2: e.tensor_copy(
            g_sbuf[:, c0:c1], pq[2 + b2][0:NH, 0:c1 - c0]),
            wait=[(sT, sT.n)], inc=sV)
    P.op("sync", lambda e: e.dma_start(gscr, g_sbuf[:]), wait=[(sV, sV.n)],
         inc=sDx, inc_n=16)
    gscr_done = sDx.n
    P.op("vector", lambda e: e.scalar_tensor_tensor(abst_sb[:], tbl_sb[:], -1.0,
                                                    tbl_sb[:], ALU.mult, ALU.max),
         wait=[(sDx, pre_ld)], inc=sV)
    P.op("tensor", lambda e: e.matmul(pq[0][0:1, 0:NB], onesf[0:NH, 0:1],
                                      abst_sb[:], start=True, stop=True),
         wait=[(sV, sV.n)], inc=sT)
    P.op("vector", lambda e: e.tensor_copy(brow_sb[0:1, 0:NB], pq[0][0:1, 0:NB]),
         wait=[(sT, sT.n)], inc=sV)
    vgap(2)
    P.op("vector", lambda e: e.tensor_reduce(bmax_sb[:], brow_sb[0:1, 0:NB],
                                             AX.X, ALU.add), inc=sV)
    brow_v = sV.n
    P.op("sync", lambda e: e.dma_start(trow_sb[0:1, 0:NH], tblT_sb[15:16, :]),
         wait=[(sV, sV.n)], inc=sDx, inc_n=16)
    P.op("sync", lambda e: e.dma_start(trow_sb[0:1, NH:2 * NH], tblT_sb[31:32, :]),
         inc=sDx, inc_n=16)
    P.op("tensor", lambda e: e.matmul(pq[0][:, 0:NH], onesf[0:1, 0:128],
                                      trow_sb[0:1, 0:NH], start=True, stop=True),
         wait=[(sDx, sDx.n), (sV, brow_v)], inc=sT)
    P.op("tensor", lambda e: e.matmul(pq[1][:, 0:NH], onesf[0:1, 0:128],
                                      trow_sb[0:1, NH:2 * NH], start=True, stop=True),
         inc=sT)
    P.op("vector", lambda e: e.tensor_copy(t15c[:], pq[0][:, 0:NH]),
         wait=[(sT, sT.n)], inc=sV)
    P.op("vector", lambda e: e.tensor_copy(t31c[:], pq[1][:, 0:NH]), inc=sV)
    P.op("vector", lambda e: e.tensor_scalar_mul(bqs_sb[:], bqkv_sb[:, 0:16], ISQ8),
         wait=[(sDx, pre_ld)], inc=sV)
    ot_last_store = [0, 0]

    def emit_ln(extra=()):
        sq_hist = []
        for qh in range(2):
            q0 = qh * 1024
            qh_v = sV.n
            for dc in range(NDC):
                half = dc % 2
                idx = qh * NDC + dc
                w = [(sV, qh_v)] + list(extra)
                if idx >= 2:
                    w.append((sT, sq_hist[idx - 2]))
                P.op("scalar", lambda e, dc=dc, q0=q0, half=half: e.activation(
                    sq_sb[:, half * 1024:(half + 1) * 1024],
                    x_sb[:, dc, q0:q0 + 1024], AF.Square), wait=w, inc=sS)
                sqv = sS.n
                for cc in range(2):
                    P.op("tensor", lambda e, dc=dc, q0=q0, cc=cc: e.matmul(
                        pq[0][0:1, cc * 512:(cc + 1) * 512], onesf[:, 0:1],
                        x_sb[:, dc, q0 + cc * 512:q0 + (cc + 1) * 512],
                        start=(dc == 0), stop=(dc == NDC - 1)),
                        wait=[(sV, qh_v)] if (dc == 0 and cc == 0) else (),
                        inc=None)
                    P.op("tensor", lambda e, half=half, cc=cc, dc=dc: e.matmul(
                        pq[1][0:1, cc * 512:(cc + 1) * 512], onesb[:, 0:1],
                        sq_sb[:, half * 1024 + cc * 512:half * 1024 + (cc + 1) * 512],
                        start=(dc == 0), stop=(dc == NDC - 1)),
                        wait=[(sS, sqv)] if cc == 0 else (),
                        inc=sT if cc == 1 else None)
                sq_hist.append(sT.n)
            tsum = sT.n
            P.op("vector", lambda e: e.tensor_scalar_mul(m_row[:], pq[0][0:1, :],
                                                         1.0 / D),
                 wait=[(sT, tsum)], inc=sV)
            P.op("vector", lambda e: e.tensor_scalar(v_row[:], pq[1][0:1, :],
                                                     1.0 / D, EPS_LN,
                                                     ALU.mult, ALU.add), inc=sV)
            P.op("vector", lambda e: e.tensor_mul(t_row[:], m_row[:], m_row[:]),
                 inc=sV)
            vgap()
            P.op("vector", lambda e: e.tensor_sub(v_row[:], v_row[:], t_row[:]),
                 inc=sV)
            vgap()
            P.op("vector", lambda e: e.reciprocal(t_row[:], v_row[:]), inc=sV)
            vgap(2)
            stat_v = sV.n
            P.op("scalar", lambda e: e.activation(v_row[:], t_row[:], AF.Sqrt),
                 wait=[(sV, stat_v)], inc=sS)
            stat_s = sS.n
            for cc in range(2):
                P.op("tensor", lambda e, cc=cc: e.matmul(
                    pq[0][:, cc * 512:(cc + 1) * 512], onesf[0:1, 0:128],
                    m_row[0:1, cc * 512:(cc + 1) * 512], start=True, stop=True),
                    wait=[(sV, stat_v), (sS, stat_s)] if cc == 0 else (), inc=None)
                P.op("tensor", lambda e, cc=cc: e.matmul(
                    pq[1][:, cc * 512:(cc + 1) * 512], onesf[0:1, 0:128],
                    v_row[0:1, cc * 512:(cc + 1) * 512], start=True, stop=True),
                    inc=sT if cc == 1 else None)
            tb = sT.n
            for dc in range(NDC):
                half = dc % 2
                P.op("vector", lambda e, dc=dc, q0=q0, half=half: e.tensor_sub(
                    sq_sb[:, half * 1024:(half + 1) * 1024],
                    x_sb[:, dc, q0:q0 + 1024], pq[0][:]),
                    wait=[(sT, tb)], inc=sV)
                P.op("vector", lambda e, dc=dc, q0=q0, half=half: e.tensor_mul(
                    u_sb[:, dc, q0:q0 + 1024],
                    sq_sb[:, half * 1024:(half + 1) * 1024], pq[1][:]),
                    inc=sV)

    for b in range(BLOC):
        xtm_free = [0, 0]
        for qch in range(16):
            i = qch % 2
            P.op("sync", lambda e, b=b, qch=qch, i=i: e.dma_start(
                xtm_sb[i][:], x_d[b, qch * 128:(qch + 1) * 128, :]),
                wait=[(sT, xtm_free[i]), (sDx, ot_last_store[i])],
                inc=sDx, inc_n=16)
            ld = sDx.n
            for grp in range(2):
                pb = grp
                for s4 in range(4):
                    dsl = grp * 4 + s4
                    P.op("tensor", lambda e, i=i, dsl=dsl, pb=pb, s4=s4: e.matmul(
                        pq[pb][:, s4 * 128:(s4 + 1) * 128],
                        xtm_sb[i][:, dsl * 128:(dsl + 1) * 128], idf[:],
                        start=True, stop=True),
                        wait=[(sDx, ld), (sV, sV.n)] if s4 == 0 else (),
                        inc=sT if s4 == 3 else None)
                P.op("vector", lambda e, b=b, qch=qch, grp=grp, pb=pb: e.tensor_copy(
                    x_sb[:, grp * 4:(grp + 1) * 4, qch * 128:(qch + 1) * 128],
                    pq[pb][:, 0:512].rearrange("p (c q) -> p c q", c=4)),
                    wait=[(sT, sT.n)], inc=sV)
            xtm_free[i] = sT.n
        if stage == "load":
            dump("x", x_sb[:], [(sV, sV.n)])
            break

        emit_ln(extra=[(sDx, gscr_done)] if b == 0 else ())
        if stage == "ln1":
            dump("u", u_sb[:], [(sV, sV.n)])
            break

        wq_ready = [0, 0]
        wq_user = [0, 0]

        def wqkv_dma(hp, gate):
            i = hp % 2
            for part in range(3):
                P.op("gpsimd", lambda e, hp=hp, i=i, part=part: e.dma_start(
                    wqkv_sb[i][:, :, part, :],
                    bass.AP(wqkv_d.tensor, part * D + hp * 128,
                            [[3 * D, 128], [128 * 3 * D, NDC], [1, 128]])),
                    wait=[(sT, max(wq_user[i], gate))] if part == 0 else (),
                    inc=sDw, inc_n=16)
            wq_ready[i] = sDw.n

        batch_gate = sT.n
        for hl in range(2):
            for kc in range(16):
                P.op("vector", lambda e, hl=hl, kc=kc: e.memset(
                    vaug_sb[hl][:, kc * 65 + 64: kc * 65 + 65], 1.0),
                    wait=[(sT, batch_gate)] if (hl == 0 and kc == 0) else (),
                    inc=sV)
        wqkv_dma(0, batch_gate)
        npair_run = 1 if stage in ("attn1", "attn2", "attn4") else NPAIR
        for hp in range(npair_run):
            i = hp % 2
            if hp + 1 < npair_run:
                wqkv_dma(hp + 1, batch_gate)
            pair_v = sV.n
            P.op("vector", lambda e: e.memset(kpz_sb[0][64:128, :], 0.0),
                 wait=[(sT, sT.n)], inc=sV)
            P.op("vector", lambda e: e.memset(kpz_sb[1][0:64, :], 0.0), inc=sV)
            pair_v = sV.n
            qkv_evac = []
            combo = 0
            for part, dst in ((0, qp_sb), (1, None), (2, vp_sb)):
                for qc in range(4):
                    slot = combo % 4
                    pb, off = slot // 2, (slot % 2) * 512
                    w = [(sDw, wq_ready[i]), (sV, pair_v)]
                    if combo >= 4:
                        w.append((sV, qkv_evac[combo - 4]))
                    for dc in range(NDC):
                        P.op("tensor", lambda e, part=part, qc=qc, dc=dc, pb=pb,
                             off=off, i=i: e.matmul(
                                 pq[pb][:, off:off + 512],
                                 wqkv_sb[i][:, dc, part, :],
                                 u_sb[:, dc, qc * 512:(qc + 1) * 512],
                                 start=(dc == 0), stop=(dc == NDC - 1)),
                             wait=w if dc == 0 else (),
                             inc=sT if dc == NDC - 1 else None)
                    if part == 0:
                        P.op("vector", lambda e, qc=qc, pb=pb, off=off,
                             bcol=bqs_sb[:, hp:hp + 1]: e.tensor_scalar(
                                 qp_sb[:, qc * 512:(qc + 1) * 512],
                                 pq[pb][:, off:off + 512], bcol, ISQ8,
                                 ALU.add, ALU.mult),
                             wait=[(sT, sT.n)], inc=sV)
                    elif part == 1:
                        bcol = bqs_sb[:, 8 + hp:8 + hp + 1]
                        for hl2 in range(2):
                            P.op("vector", lambda e, qc=qc, pb=pb, off=off,
                                 bcol=bcol, hl2=hl2: e.tensor_scalar(
                                     kpz_sb[hl2][hl2 * 64:hl2 * 64 + 64,
                                                 qc * 512:(qc + 1) * 512],
                                     pq[pb][hl2 * 64:hl2 * 64 + 64,
                                            off:off + 512],
                                     bcol[hl2 * 64:hl2 * 64 + 64, :], ISQ8,
                                     ALU.add, ALU.mult),
                                 wait=[(sT, sT.n)] if hl2 == 0 else (), inc=sV)
                    else:
                        P.op("vector", lambda e, qc=qc, pb=pb, off=off,
                             bcol=bqkv_sb[:, 16 + hp:16 + hp + 1]: e.tensor_scalar(
                                 vp_sb[:, qc * 512:(qc + 1) * 512],
                                 pq[pb][:, off:off + 512], bcol, None, ALU.add),
                             wait=[(sT, sT.n)], inc=sV)
                    qkv_evac.append(sV.n)
                    combo += 1
            wq_user[i] = sT.n
            for src, dst2 in ((qp_sb, mq2_sb), (kpz_sb, mk2_sb)):
                if src is kpz_sb:
                    P.op("vector", lambda e: e.tensor_mul(
                        sq2_sb[0:64, :], kpz_sb[0][0:64, :], kpz_sb[0][0:64, :]),
                        wait=[(sV, sV.n), (sT, sT.n)], inc=sV)
                    P.op("vector", lambda e: e.tensor_mul(
                        sq2_sb[64:128, :], kpz_sb[1][64:128, :],
                        kpz_sb[1][64:128, :]), inc=sV)
                else:
                    P.op("vector", lambda e, src=src: e.tensor_mul(
                        sq2_sb[:], src[:], src[:]),
                        wait=[(sV, sV.n), (sT, sT.n)], inc=sV)
                vv = sV.n
                for cc in range(4):
                    P.op("tensor", lambda e, cc=cc: e.matmul(
                        pq[cc // 2][0:1, (cc % 2) * 512:(cc % 2) * 512 + 512],
                        onesb[:, 0:1], sq2_sb[:, cc * 512:(cc + 1) * 512],
                        start=True, stop=True),
                        wait=[(sV, vv)] if cc == 0 else (),
                        inc=sT if cc == 3 else None)
                P.op("vector", lambda e, dst2=dst2: e.tensor_reduce(
                    dst2[0:1, 0:1], pq[0][0:1, :], AX.X, ALU.max),
                    wait=[(sT, sT.n)], inc=sV)
                P.op("vector", lambda e, dst2=dst2: e.tensor_reduce(
                    dst2[0:1, 1:2], pq[1][0:1, :], AX.X, ALU.max), inc=sV)
                vgap(2)
                P.op("vector", lambda e, dst2=dst2: e.tensor_tensor(
                    dst2[0:1, 0:1], dst2[0:1, 0:1], dst2[0:1, 1:2], ALU.max),
                    inc=sV)
                vgap(2)
            P.op("vector", lambda e: e.tensor_scalar(
                cc_sb[:], mq2_sb[0:1, 0:1], mk2_sb[0:1, 0:1], 0.5,
                ALU.add, ALU.mult), inc=sV)
            vgap(2)
            P.op("vector", lambda e: e.tensor_scalar(
                cc_sb[:], cc_sb[:], bmax_sb[0:1, 0:1], None, ALU.add), inc=sV)
            vgap(2)
            P.op("tensor", lambda e: e.matmul(pq[0][:, 0:1], onesf[0:1, 0:128],
                                              cc_sb[:], start=True, stop=True),
                 wait=[(sV, sV.n)], inc=sT)
            P.op("vector", lambda e: e.tensor_scalar_mul(
                negc_sb[:], pq[0][:, 0:1], -1.0), wait=[(sT, sT.n)], inc=sV)
            for hl in range(2):
                h = 2 * hp + hl
                P.op("vector", lambda e, hl=hl, h=h: e.tensor_sub(
                    t31mc[hl][:], t31c[:, h:h + 1], pq[0][:, 0:1]), inc=sV)
                P.op("vector", lambda e, hl=hl, h=h: e.tensor_sub(
                    t15mc[hl][:], t15c[:, h:h + 1], pq[0][:, 0:1]), inc=sV)
            caps_done = sV.n
            for hl in range(2):
                h = 2 * hp + hl
                P.op("sync", lambda e, hl=hl, h=h: e.dma_start(
                    band_sb[hl][:],
                    bass.AP(gscr.tensor, h * GLEN + (S - 1) - (BW + 127),
                            [[1, 128], [1, BANDW]])),
                    wait=[(sDx, gscr_done), (sS, sS.n)], inc=sDx, inc_n=16)
            band_done = sDx.n
            for hl in range(2):
                for kc in range(16):
                    P.op("tensor", lambda e, hl=hl, kc=kc: e.matmul(
                        pq[2 + hl][:, kc * 64:(kc + 1) * 64],
                        vp_sb[hl * 64:hl * 64 + 64, kc * 128:(kc + 1) * 128],
                        idb[hl * 64:hl * 64 + 64, hl * 64:hl * 64 + 64],
                        start=True, stop=True, tile_position=(hl * 64, 0)),
                        wait=[(sV, caps_done)] if kc == 0 else (),
                        inc=sT if kc == 15 else None)
            for hl in range(2):
                P.op("vector", lambda e, hl=hl: e.tensor_copy(
                    vaug_sb[hl][:].rearrange("p (k j) -> p k j", j=65)[:, :, 0:64],
                    pq[2 + hl][:].rearrange("p (k j) -> p k j", j=64)),
                    wait=[(sT, sT.n)], inc=sV)
            qh_gate = sV.n
            s_guard = [sS.n] * 4
            pt_guard = [[sT.n, sT.n], [sT.n, sT.n]]
            for qh in range(1 if stage in ("attn2", "attn4") else 2):
                q0 = qh * 1024
                pv_pend = []
                it = 0
                for kc in range(16):
                    for hl in range(2):
                        par = kc % 2
                        for ch in range(2):
                            slot = it % 4
                            spq, soff = pq[slot // 2], (slot % 2) * 512
                            qa = q0 + ch * 512
                            k0 = kc * 128
                            Lc = min(max(k0 - BW - qa, 0), 512)
                            Rc = min(max(k0 + BW + 128 - qa, 0), 512)
                            P.op("tensor", lambda e, hl=hl, kc=kc, qa=qa, spq=spq,
                                 soff=soff: e.matmul(
                                     spq[:, soff:soff + 512],
                                     kpz_sb[hl][:, kc * 128:(kc + 1) * 128],
                                     qp_sb[:, qa:qa + 512],
                                     start=True, stop=True),
                                 wait=[(sS, s_guard[slot]), (sV, qh_gate)],
                                 inc=sT)
                            qk_t = sT.n
                            bnd_v = None
                            if Rc > Lc:
                                jf = k0 + BW + 127 - (qa + Lc)
                                jl = jf - (Rc - Lc) + 1
                                stop = jl - 1 if jl > 0 else None
                                P.op("vector", lambda e, hl=hl, Lc=Lc, Rc=Rc,
                                     jf=jf, stop=stop, spq=spq, soff=soff:
                                     e.tensor_tensor(
                                         spq[:, soff + Lc:soff + Rc],
                                         spq[:, soff + Lc:soff + Rc],
                                         band_sb[hl][:, jf:stop:-1], ALU.add),
                                     wait=[(sT, qk_t), (sDx, band_done)], inc=sV)
                                bnd_v = sV.n
                            we = [(sT, qk_t), (sT, pt_guard[hl][par])]
                            if bnd_v is not None:
                                we.append((sV, bnd_v))
                            regs = []
                            if Lc > 0:
                                regs.append((0, Lc, t31mc[hl]))
                            if Rc > Lc:
                                regs.append((Lc, Rc, negc_sb))
                            if Rc < 512:
                                regs.append((Rc, 512, t15mc[hl]))
                            for ri, (a, bb, bias) in enumerate(regs):
                                P.op("scalar", lambda e, hl=hl, par=par, a=a,
                                     bb=bb, bias=bias, ch=ch, spq=spq, soff=soff:
                                     e.activation(
                                         pt_sb[hl][par][:, ch * 512 + a:
                                                        ch * 512 + bb],
                                         spq[:, soff + a:soff + bb],
                                         AF.Exp, bias=bias[:, 0:1], scale=1.0),
                                     wait=we if ri == 0 else (),
                                     inc=sS if ri == len(regs) - 1 else None)
                            s_guard[slot] = sS.n
                            pv_pend.append((kc, hl, par, ch, sS.n))
                            if len(pv_pend) > 2:
                                kc2, hl2, par2, ch2, g2 = pv_pend.pop(0)
                                P.op("tensor", lambda e, hl=hl2, kc=kc2, par=par2,
                                     ch=ch2: e.matmul(
                                         pq[2 + hl][0:65, ch * 512:(ch + 1) * 512],
                                         vaug_sb[hl][:, kc * 65:kc * 65 + 65],
                                         pt_sb[hl][par][:, ch * 512:(ch + 1) * 512],
                                         start=(kc == 0), stop=(kc == 15)),
                                     wait=[(sS, g2)], inc=sT)
                                pt_guard[hl2][par2] = sT.n
                            it += 1
                for kc2, hl2, par2, ch2, g2 in pv_pend:
                    P.op("tensor", lambda e, hl=hl2, kc=kc2, par=par2, ch=ch2:
                         e.matmul(pq[2 + hl][0:65, ch * 512:(ch + 1) * 512],
                                  vaug_sb[hl][:, kc * 65:kc * 65 + 65],
                                  pt_sb[hl][par][:, ch * 512:(ch + 1) * 512],
                                  start=(kc == 0), stop=(kc == 15)),
                         wait=[(sS, g2)], inc=sT)
                    pt_guard[hl2][par2] = sT.n
                pv_pend = []
                nhl = 1 if stage == "attn2" else 2
                ev1 = [0, 0]
                for hl in range(nhl):
                    P.op("vector", lambda e, hl=hl: e.tensor_copy(
                        ctxu2[hl][0:65, :], pq[2 + hl][0:65, :]),
                        wait=[(sT, pt_guard[hl][1]), (sT, pt_guard[hl][0])], inc=sV)
                    ev1[hl] = sV.n
                tzz = [0, 0]
                for hl in range(nhl):
                    for ch in range(8):
                        P.op("tensor", lambda e, ch=ch, hl=hl: e.matmul(
                            pq[hl][:, 512 + ch:512 + ch + 1],
                            ctxu2[hl][64:65, ch * 128:(ch + 1) * 128],
                            onesb[64:65, 0:1], start=True, stop=True),
                            wait=[(sV, ev1[hl])] if ch == 0 else (),
                            inc=sT if ch == 7 else None)
                    tzz[hl] = sT.n
                for hl in range(nhl):
                    P.op("vector", lambda e, hl=hl: e.reciprocal(
                        rz2[hl][:], pq[hl][:, 512:520]),
                        wait=[(sT, tzz[hl])], inc=sV)
                tz2 = [0, 0]
                for hl in range(nhl):
                    for ch in range(8):
                        P.op("tensor", lambda e, ch=ch, hl=hl: e.matmul(
                            pq[hl][:, ch * 64:(ch + 1) * 64],
                            ctxu2[hl][0:64, ch * 128:(ch + 1) * 128],
                            idb[0:64, 0:64], start=True, stop=True),
                            inc=sT if ch == 7 else None)
                    tz2[hl] = sT.n
                ev2 = [0, 0]
                for hl in range(nhl):
                    P.op("vector", lambda e, hl=hl: e.tensor_tensor(
                        ctxtm2[hl][:].rearrange("p (c j) -> p c j", j=64),
                        pq[hl][:, 0:512].rearrange("p (c j) -> p c j", j=64),
                        rz2[hl][:].rearrange("p (c o) -> p c o", o=1)
                        .broadcast_to([128, 8, 64]), ALU.mult),
                        wait=[(sT, tz2[hl])], inc=sV)
                    ev2[hl] = sV.n
                for hl in range(nhl):
                    for ch in range(8):
                        P.op("tensor", lambda e, ch=ch, hl=hl: e.matmul(
                            pq[hl][0:64, ch * 128:(ch + 1) * 128],
                            ctxtm2[hl][:, ch * 64:(ch + 1) * 64], idb[:],
                            start=True, stop=True),
                            wait=[(sV, ev2[hl])] if ch == 0 else (),
                            inc=sT if ch == 7 else None)
                    tf = sT.n
                    P.op("vector", lambda e, hl=hl, hp=hp, q0=q0: e.tensor_copy(
                        ctx_sb[hl * 64:hl * 64 + 64, hp, q0:q0 + 1024],
                        pq[hl][0:64, :]),
                        wait=[(sT, tf)], inc=sV)
                qh_gate = sV.n
                s_guard = [sS.n] * 4
                pt_guard = [[sT.n, sT.n], [sT.n, sT.n]]
        if stage == "attn":
            dump("ctx", ctx_sb[:], [(sV, sV.n)])
            break
        if stage == "attn4":
            dump("t2", sq2_sb[:, 0:512], [(sV, sV.n)])
            dump("rz", rz2[1][:], [(sV, sV.n)])
            dump("ctxtm", ctxtm2[1][:], [(sV, sV.n)])
            dump("ctxu", ctxu2[1][:], [(sV, sV.n)])
            dump("ctx", ctx_sb[:], [(sV, sV.n)])
            dump("qp", qp_sb[:], [(sV, sV.n)])
            dump("kp", kpz_sb[0][:], [(sV, sV.n)])
            dump("vp", vp_sb[:], [(sV, sV.n)])
            dump("negc", negc_sb[:], [(sV, sV.n)])
            break
        if stage == "attn2":
            dump("mq2", mq2_sb[:], [(sV, sV.n)])
            dump("mk2", mk2_sb[:], [(sV, sV.n)])
            dump("ccv", cc_sb[:], [(sV, sV.n)])
            dump("bmax", bmax_sb[:], [(sV, sV.n)])
            dump("brow", brow_sb[:], [(sV, sV.n)])
            dump("t31c", t31c[:], [(sV, sV.n)])
            dump("t15c", t15c[:], [(sV, sV.n)])
            dump("t31mc0", t31mc[0][:], [(sV, sV.n)])
            dump("t15mc0", t15mc[0][:], [(sV, sV.n)])
            dump("t31mc1", t31mc[1][:], [(sV, sV.n)])
            dump("sq2", sq2_sb[:], [(sV, sV.n)])
            dump("ctxu", ctxu2[0][:], [(sV, sV.n)])
            dump("ctx", ctx_sb[:], [(sV, sV.n)])
            dump("pt00", pt_sb[0][0][:], [(sV, sV.n)])
            dump("pt01", pt_sb[0][1][:], [(sV, sV.n)])
            dump("qp", qp_sb[:], [(sV, sV.n)])
            dump("kp", kpz_sb[0][:], [(sV, sV.n)])
            dump("vp", vp_sb[:], [(sV, sV.n)])
            dump("negc", negc_sb[:], [(sV, sV.n)])
            break
        if stage == "attn1":
            dump("ctx", ctx_sb[:], [(sV, sV.n)])
            dump("qp", qp_sb[:], [(sV, sV.n)])
            dump("kp", kpz_sb[0][:], [(sV, sV.n)])
            dump("vp", vp_sb[:], [(sV, sV.n)])
            dump("vaug0", vaug_sb[0][:], [(sV, sV.n)])
            dump("pt00", pt_sb[0][0][:], [(sV, sV.n)])
            dump("pt01", pt_sb[0][1][:], [(sV, sV.n)])
            dump("pt10", pt_sb[1][0][:], [(sV, sV.n)])
            dump("negc", negc_sb[:], [(sV, sV.n)])
            dump("t31mc0", t31mc[0][:], [(sV, sV.n)])
            dump("t15mc0", t15mc[0][:], [(sV, sV.n)])
            dump("band0", band_sb[0][:], [(sV, sV.n)])
            break

        P.op("gpsimd", lambda e: e.dma_start(
            wdf_sb[:], wd_d.rearrange("(pr p) d -> p pr d", p=128)),
            wait=[(sT, sT.n)], inc=sDw, inc_n=16)
        wd_ready = sDw.n
        prj_v = sV.n
        for qc in range(4):
            for pair in range(NPAIR):
                for do in range(NDC):
                    P.op("tensor", lambda e, pair=pair, do=do, qc=qc: e.matmul(
                        pq[do // 2][:, (do % 2) * 512:(do % 2) * 512 + 512],
                        wdf_sb[:, pair, do * 128:(do + 1) * 128],
                        ctx_sb[:, pair, qc * 512:(qc + 1) * 512],
                        start=(pair == 0), stop=(pair == NPAIR - 1)),
                        wait=[(sDw, wd_ready), (sV, prj_v)]
                        if (pair == 0 and do == 0) else (),
                        inc=sT if (pair == NPAIR - 1 and do == NDC - 1) else None)
            tprj = sT.n
            for do in range(NDC):
                P.op("vector", lambda e, do=do, qc=qc: e.scalar_tensor_tensor(
                    x_sb[:, do, qc * 512:(qc + 1) * 512],
                    pq[do // 2][:, (do % 2) * 512:(do % 2) * 512 + 512],
                    bd_sb[:, do:do + 1], x_sb[:, do, qc * 512:(qc + 1) * 512],
                    ALU.add, ALU.add),
                    wait=[(sT, tprj)], inc=sV)
            prj_v = sV.n
        if stage == "proj":
            dump("out1", x_sb[:], [(sV, sV.n)])
            break

        emit_ln()
        if stage == "ln2":
            dump("u2", u_sb[:], [(sV, sV.n)])
            break

        lg0 = sV.n
        for qh2 in range(2):
            for cc in range(2):
                for dc in range(NDC):
                    P.op("tensor", lambda e, qh2=qh2, cc=cc, dc=dc: e.matmul(
                        pq[qh2][0:E, cc * 512:(cc + 1) * 512],
                        wg_sb[:, dc * E:(dc + 1) * E],
                        u_sb[:, dc, qh2 * 1024 + cc * 512:qh2 * 1024 + (cc + 1) * 512],
                        start=(dc == 0), stop=(dc == NDC - 1)),
                        wait=[(sV, lg0), (sDw, 16)] if dc == 0 else (),
                        inc=sT if (dc == NDC - 1 and cc == 1) else None)
        for qh2 in range(2):
            P.op("vector", lambda e, qh2=qh2: e.tensor_copy(
                logits_sb[:, qh2 * 1024:(qh2 + 1) * 1024], pq[qh2][0:E, :]),
                wait=[(sT, sT.n)], inc=sV)
        lge = sV.n
        for ch in range(16):
            P.op("tensor", lambda e, ch=ch: e.matmul(
                pq[2][:, ch * 8:(ch + 1) * 8],
                logits_sb[:, ch * 128:(ch + 1) * 128], idf[0:E, 0:E],
                start=True, stop=True),
                wait=[(sV, lge)] if ch == 0 else (), inc=sT if ch == 15 else None)
        P.op("vector", lambda e: e.tensor_copy(lg_tm[:], pq[2][:, 0:128]),
             wait=[(sT, sT.n)], inc=sV)
        vgap(2)
        lg3 = lg_tm[:].rearrange("p (c j) -> p c j", j=8)
        m13 = m1_sb[:].rearrange("p (c o) -> p c o", o=1)
        m23 = m2_sb[:].rearrange("p (c o) -> p c o", o=1)
        pex3 = pexp_sb[:].rearrange("p (c j) -> p c j", j=8)
        P.op("vector", lambda e: e.tensor_reduce(m13, lg3, AX.X, ALU.max), inc=sV)
        vgap(2)
        P.op("vector", lambda e: e.tensor_tensor(
            pex3, lg3, m13.broadcast_to([128, 16, 8]), ALU.subtract), inc=sV)
        P.op("scalar", lambda e: e.activation(pexp_sb[:], pexp_sb[:], AF.Exp),
             wait=[(sV, sV.n)], inc=sS)
        P.op("vector", lambda e: e.tensor_reduce(m13, pex3, AX.X, ALU.max),
             wait=[(sS, sS.n)], inc=sV)
        vgap(2)
        P.op("vector", lambda e: e.tensor_tensor(
            keep_sb[:].rearrange("p (c j) -> p c j", j=8), pex3,
            m13.broadcast_to([128, 16, 8]), ALU.is_ge), inc=sV)
        vgap(2)
        P.op("vector", lambda e: e.scalar_tensor_tensor(
            q2_sb[:], keep_sb[:], -1e30, pexp_sb[:], ALU.mult, ALU.add), inc=sV)
        vgap(2)
        P.op("vector", lambda e: e.tensor_reduce(
            m23, q2_sb[:].rearrange("p (c j) -> p c j", j=8), AX.X, ALU.max),
            inc=sV)
        vgap(2)
        P.op("vector", lambda e: e.tensor_tensor(
            keep_sb[:].rearrange("p (c j) -> p c j", j=8), pex3,
            m23.broadcast_to([128, 16, 8]), ALU.is_ge), inc=sV)
        vgap(2)
        P.op("vector", lambda e: e.tensor_mul(q2_sb[:], pexp_sb[:], keep_sb[:]),
             inc=sV)
        P.op("vector", lambda e: e.tensor_reduce(
            den_sb[:].rearrange("p (c o) -> p c o", o=1), pex3, AX.X, ALU.add),
            inc=sV)
        vgap(2)
        P.op("vector", lambda e: e.scalar_tensor_tensor(
            den_sb[:], den_sb[:], 1e-8, m2_sb[:], ALU.mult, ALU.add), inc=sV)
        vgap(2)
        P.op("vector", lambda e: e.tensor_add(den_sb[:], den_sb[:], m1_sb[:]),
             inc=sV)
        vgap(2)
        P.op("vector", lambda e: e.reciprocal(rden_sb[:], den_sb[:]), inc=sV)
        vgap(2)
        P.op("vector", lambda e: e.tensor_tensor(
            cval_sb[:].rearrange("p (c j) -> p c j", j=8),
            q2_sb[:].rearrange("p (c j) -> p c j", j=8),
            rden_sb[:].rearrange("p (c o) -> p c o", o=1).broadcast_to([128, 16, 8]),
            ALU.mult), inc=sV)
        P.op("scalar", lambda e: e.activation(sqctm_sb[:], cval_sb[:], AF.Sqrt),
             wait=[(sV, sV.n)], inc=sS)
        sqv = sS.n
        for ch in range(16):
            pb = 2 + (ch // 8)
            P.op("tensor", lambda e, ch=ch, pb=pb: e.matmul(
                pq[pb][0:E, (ch % 8) * 128:(ch % 8 + 1) * 128],
                sqctm_sb[:, ch * 8:(ch + 1) * 8], idb[:],
                start=True, stop=True),
                wait=[(sS, sqv)] if ch == 0 else (),
                inc=sT if ch in (7, 15) else None)
        for half in range(2):
            P.op("vector", lambda e, half=half: e.tensor_copy(
                sqce_sb[:, half * 1024:(half + 1) * 1024], pq[2 + half][0:E, :]),
                wait=[(sT, sT.n)], inc=sV)
        if stage == "gate":
            dump("sqce", sqce_sb[:], [(sV, sV.n)])
            break

        def w1_dma(e_idx, wait_t):
            P.op("gpsimd", lambda e, e_idx=e_idx: e.dma_start(
                w1sl[:], w1_d[e_idx].rearrange("(dc p) h -> p dc h", p=128)),
                wait=[(sT, wait_t)], inc=sDw, inc_n=16)
            return sDw.n

        def w2_dma(e_idx, wait_t):
            P.op("gpsimd", lambda e, e_idx=e_idx: e.dma_start(
                w2sl[:], w2_d[e_idx].rearrange("(jc p) d -> p jc d", p=128)),
                wait=[(sT, wait_t)], inc=sDw, inc_n=16)
            return sDw.n

        w1_ready = w1_dma(0, sT.n)
        w2_ready = w2_dma(0, sT.n)
        y_done = sV.n
        for ei in range(E):
            sq0 = sV.n
            for qc in range(4):
                P.op("tensor", lambda e, ei=ei, qc=qc: e.matmul(
                    pq[qc // 2][:, (qc % 2) * 512:(qc % 2) * 512 + 512],
                    sel8[:, ei * 128:(ei + 1) * 128],
                    sqce_sb[:, qc * 512:(qc + 1) * 512], start=True, stop=True),
                    wait=[(sV, sq0)] if qc == 0 else (),
                    inc=sT if qc == 3 else None)
            for half in range(2):
                P.op("vector", lambda e, half=half: e.tensor_copy(
                    sqcb_sb[:, half * 1024:(half + 1) * 1024], pq[half][:, 0:1024]),
                    wait=[(sT, sT.n)], inc=sV)
            sqcb_done = sV.n
            ue_done = None
            for qc in range(4):
                if qc == 0:
                    ue_t = sT.n
                    for dc in range(NDC):
                        P.op("vector", lambda e, dc=dc, qc=qc: e.tensor_mul(
                            ue_sb[:, dc, :], u_sb[:, dc, qc * 512:(qc + 1) * 512],
                            sqcb_sb[:, qc * 512:(qc + 1) * 512]),
                            wait=[(sV, sqcb_done), (sT, ue_t)] if dc == 0 else (),
                            inc=sV)
                    ue_done = sV.n
                relu_v = {}
                g_v = {}
                for hc in range(NHC):
                    slot = hc % 4
                    pb, off = slot // 2, (slot % 2) * 512
                    if hc < 4:
                        w = [(sDw, w1_ready), (sV, ue_done), (sV, y_done)]
                    else:
                        w = [(sV, relu_v[hc - 4])]
                    for dc in range(NDC):
                        P.op("tensor", lambda e, hc=hc, dc=dc, pb=pb, off=off:
                             e.matmul(pq[pb][:, off:off + 512],
                                      w1sl[:, dc, hc * 128:(hc + 1) * 128],
                                      ue_sb[:, dc, :],
                                      start=(dc == 0), stop=(dc == NDC - 1)),
                             wait=w if dc == 0 else (),
                             inc=sT if dc == NDC - 1 else None)
                    P.op("vector", lambda e, pb=pb, off=off, hc=hc:
                         e.tensor_scalar_max(
                             trelu_sb2[hc % 2][:], pq[pb][:, off:off + 512], 0.0),
                         wait=[(sT, sT.n)], inc=sV)
                    relu_v[hc] = sV.n
                    if hc > 0:
                        P.op("vector", lambda e, hc=hc: e.tensor_mul(
                            gg_sb[:, hc - 1, :], trelu_sb2[(hc - 1) % 2][:],
                            trelu_sb2[(hc - 1) % 2][:]), inc=sV)
                        g_v[hc - 1] = sV.n
                P.op("vector", lambda e: e.tensor_mul(
                    gg_sb[:, NHC - 1, :], trelu_sb2[(NHC - 1) % 2][:],
                    trelu_sb2[(NHC - 1) % 2][:]), inc=sV)
                g_v[NHC - 1] = sV.n
                w1_user = sT.n
                g_done = sV.n
                if ei + 1 < E and qc == 3:
                    w1_ready = w1_dma(ei + 1, w1_user)
                if qc < 3:
                    for dc in range(NDC):
                        P.op("vector", lambda e, dc=dc, qc=qc: e.tensor_mul(
                            ue_sb[:, dc, :],
                            u_sb[:, dc, (qc + 1) * 512:(qc + 2) * 512],
                            sqcb_sb[:, (qc + 1) * 512:(qc + 2) * 512]),
                            wait=[(sT, w1_user)] if dc == 0 else (),
                            inc=sV)
                    next_ue_done = sV.n
                for do in list(range(4, NDC)) + list(range(4)):
                    for jc in range(NHC):
                        w = []
                        if do == 4:
                            w = [(sDw, w2_ready), (sV, g_v[jc])]
                        elif do == 0 and jc == 0:
                            w = [(sV, g_done)]
                        P.op("tensor", lambda e, do=do, jc=jc: e.matmul(
                            pq[do // 2][:, (do % 2) * 512:(do % 2) * 512 + 512],
                            w2sl[:, jc, do * 128:(do + 1) * 128],
                            gg_sb[:, jc, :], start=(jc == 0), stop=(jc == NHC - 1)),
                            wait=w,
                            inc=sT if jc == NHC - 1 else None)
                w2_user = sT.n
                ty = sT.n
                for do in range(NDC):
                    P.op("vector", lambda e, do=do, qc=qc: e.tensor_add(
                        x_sb[:, do, qc * 512:(qc + 1) * 512],
                        x_sb[:, do, qc * 512:(qc + 1) * 512],
                        pq[do // 2][:, (do % 2) * 512:(do % 2) * 512 + 512]),
                        wait=[(sT, ty)], inc=sV)
                y_done = sV.n
                if ei + 1 < E and qc == 3:
                    w2_ready = w2_dma(ei + 1, w2_user)
                if qc < 3:
                    ue_done = next_ue_done
        if stage == "ffn":
            dump("out2", x_sb[:], [(sV, sV.n)])
            break

        for qch in range(16):
            i = qch % 2
            for grp in range(2):
                for s4 in range(4):
                    dsl = grp * 4 + s4
                    P.op("tensor", lambda e, dsl=dsl, qch=qch, grp=grp, s4=s4:
                         e.matmul(pq[grp][:, s4 * 128:(s4 + 1) * 128],
                                  x_sb[:, dsl, qch * 128:(qch + 1) * 128], idf[:],
                                  start=True, stop=True),
                         wait=[(sV, sV.n)] if (grp == 0 and s4 == 0) else (),
                         inc=sT if s4 == 3 else None)
            tr1 = sT.n
            P.op("vector", lambda e, i=i: e.tensor_copy(
                ot_sb[i][:, 0:512], pq[0][:, 0:512]),
                wait=[(sT, tr1), (sDx, ot_last_store[i])], inc=sV)
            P.op("vector", lambda e, i=i: e.tensor_copy(
                ot_sb[i][:, 512:1024], pq[1][:, 0:512]), inc=sV)
            P.op("sync", lambda e, b=b, qch=qch, i=i: e.dma_start(
                y_d[b, qch * 128:(qch + 1) * 128, :], ot_sb[i][:]),
                wait=[(sV, sV.n)], inc=sDx, inc_n=16)
            ot_last_store[i] = sDx.n

    P.op("sync", lambda e: e.nop(), wait=[(sDx, sDx.n), (sDw, sDw.n)])

    with nc.Block() as blk:
        blk.tensor(P.emitter("tensor"))
        blk.vector(P.emitter("vector"))
        blk.scalar(P.emitter("scalar"))
        blk.sync(P.emitter("sync"))
        blk.gpsimd(P.emitter("gpsimd"))
    return nc, dbg


def _consts():
    bk = _bucket_of_rel()
    oh = np.zeros((NB, GLEN), np.float32)
    oh[bk, np.arange(GLEN)] = 1.0
    sel8 = np.zeros((E, E * 128), ml_dtypes.bfloat16)
    for e in range(E):
        sel8[e, e * 128:(e + 1) * 128] = 1.0
    return {"oh": oh,
            "cones": np.ones((128, 128), ml_dtypes.bfloat16),
            "cidb": np.eye(128, dtype=ml_dtypes.bfloat16),
            "cidf": np.eye(128, dtype=np.float32),
            "conesf": np.ones((128, 128), np.float32),
            "csel8": sel8}


_NC_CACHE = {}


def kernel(L0_attn_hidden_states, L0_attn_w_qkv, L0_attn_b_qkv, L0_attn_w_dense,
           L0_attn_b_dense, L0_attn_rel_attn_table, L0_ffn_Wg, L0_ffn_W1,
           L0_ffn_W2):
    x = np.ascontiguousarray(L0_attn_hidden_states, dtype=np.float32)
    if "full" not in _NC_CACHE:
        _NC_CACHE["full"] = build_nc("full")
    nc, _ = _NC_CACHE["full"]
    shared = _consts()
    shared.update({
        "wqkv": np.ascontiguousarray(L0_attn_w_qkv, np.float32),
        "bqkv": np.ascontiguousarray(L0_attn_b_qkv, np.float32),
        "wdense": np.ascontiguousarray(L0_attn_w_dense, np.float32),
        "bdense": np.ascontiguousarray(L0_attn_b_dense, np.float32),
        "table": np.ascontiguousarray(L0_attn_rel_attn_table, np.float32),
        "wg": np.ascontiguousarray(L0_ffn_Wg, np.float32),
        "w1": np.ascontiguousarray(L0_ffn_W1, np.float32),
        "w2": np.ascontiguousarray(L0_ffn_W2, np.float32),
    })
    in_maps = []
    for c in range(NCORES):
        m = dict(shared)
        m["x"] = np.ascontiguousarray(x[c * BLOC:(c + 1) * BLOC])
        in_maps.append(m)
    res = run_bass_kernel_spmd(nc, in_maps, core_ids=list(range(NCORES)))
    return np.concatenate([r["y"] for r in res.results], axis=0)
